# version 13
# baseline (speedup 1.0000x reference)
import numpy as np
import ml_dtypes
from contextlib import ExitStack
import concourse.bass as bass
import concourse.mybir as mybir
from concourse.bass_utils import run_bass_kernel_spmd

F32 = mybir.dt.float32
BF16 = mybir.dt.bfloat16
AX = mybir.AxisListType
AF = mybir.ActivationFunctionType
ALU = mybir.AluOpType
BF = ml_dtypes.bfloat16

S = 4096
D = 1024
DC = 8
TG = 512
NTG = S // TG
NH = 16
DH = 64
LN_EPS = 1e-5
RMS_EPS = float(np.finfo(np.float32).eps)
SQ192 = float(np.sqrt(np.float64(np.float32(192.0))))
ISQ192 = float(1.0 / np.sqrt(np.float32(192.0)))
ALPHA = 1.702
MAX_BIAS = 8.0

ENGINES = ("tensor", "vector", "scalar", "gpsimd", "sync")


def _slopes16():
    start = 2.0 ** (-(2.0 ** -(np.log2(16) - 3)))
    return np.array([start * start**i for i in range(16)], dtype=np.float64)


class Prog:

    def __init__(self, nc):
        self.nc = nc
        self.recs = {e: [] for e in ENGINES}
        self.w = {}
        self.r = {}

    @staticmethod
    def _key(x):
        return x if isinstance(x, str) else id(x)

    def op(self, eng, fn, reads=(), writes=(), extra=(), dma=False):
        waits = set(t for t in extra if t is not None)
        for x in reads:
            k = self._key(x)
            t = self.w.get(k)
            if t is not None:
                waits.add(t)
        for x in writes:
            k = self._key(x)
            t = self.w.get(k)
            if t is not None:
                waits.add(t)
            for t2 in self.r.get(k, ()):
                waits.add(t2)
        tok = (eng, len(self.recs[eng]), dma)
        self.recs[eng].append([fn, tuple(waits), dma, False])
        for t in waits:
            if t[0] != eng or t[2] or eng != "tensor":
                self.recs[t[0]][t[1]][3] = True
        for x in reads:
            self.r.setdefault(self._key(x), []).append(tok)
        for x in writes:
            k = self._key(x)
            self.w[k] = tok
            self.r[k] = []
        return tok

    def dma(self, eng, fn, reads=(), writes=(), extra=()):
        return self.op(eng, fn, reads, writes, extra, dma=True)

    def barrier(self):
        lasts = []
        for e in ENGINES:
            if self.recs[e]:
                i = len(self.recs[e]) - 1
                lasts.append((e, i, self.recs[e][i][2]))
        for e in ENGINES:
            w = tuple(t for t in lasts if t[0] != e or t[2])
            tok = self.op(e, lambda eng: eng.nop(), extra=w)
        bar = [(e, len(self.recs[e]) - 1, False) for e in ENGINES]
        for k in list(self.w.keys()):
            self.w[k] = bar[0]
            self.r[k] = [bar[ENGINES.index(e)] for e in ENGINES[1:]]
        return bar

    def finalize(self, ctx):
        nc = self.nc
        sems = {}
        for e in ENGINES:
            sems[(e, False)] = ctx.enter_context(nc.semaphore(f"sem_{e}_c"))
            sems[(e, True)] = ctx.enter_context(nc.semaphore(f"sem_{e}_d"))
        val = {}
        for e in ENGINES:
            c_cnt = 0
            d_cnt = 0
            for i, rec in enumerate(self.recs[e]):
                if rec[2]:
                    d_cnt += 16
                    val[(e, i)] = ((e, True), d_cnt)
                elif rec[3]:
                    c_cnt += 1
                    val[(e, i)] = ((e, False), c_cnt)
        block = ctx.enter_context(nc.Block())

        def make_section(e):
            recs = self.recs[e]

            def section(eng):
                hw = {}
                for i, (fn, waits, is_dma, marked) in enumerate(recs):
                    need = {}
                    for t in waits:
                        if t[0] == e and not t[2] and e == "tensor":
                            continue
                        sk, v = val[(t[0], t[1])]
                        if need.get(sk, 0) < v:
                            need[sk] = v
                    for sk in sorted(need, key=lambda s: (s[0], s[1])):
                        v = need[sk]
                        if hw.get(sk, 0) < v:
                            eng.wait_ge(sems[sk], v)
                            hw[sk] = v
                    inst = fn(eng)
                    if is_dma:
                        inst.then_inc(sems[(e, True)], 16)
                    elif marked:
                        inst.then_inc(sems[(e, False)], 1)

            return section

        for e in ENGINES:
            if self.recs[e]:
                getattr(block, e)(make_section(e))

    def stats(self):
        return {e: len(self.recs[e]) for e in ENGINES}


def col_ap(hbm, tg, width=TG, nd=DC):
    a = hbm.rearrange("(c p) t -> p c t", p=128)
    return a[:, :, tg * width:(tg + 1) * width]


class NS:
    def __init__(self, **kw):
        self.__dict__.update(kw)



def host_prep(i):
    d = {}
    f32 = np.float32
    d["w_qkv0"] = np.ascontiguousarray(i["L0_attn_w_qkv"].reshape(D, NH * 192))
    d["b_qkv0"] = np.ascontiguousarray(np.asarray(i["L0_attn_b_qkv"]).reshape(1, NH * 192))
    d["w_out0"] = np.ascontiguousarray(i["L0_attn_w_out"].reshape(D, D))
    d["b_out0"] = np.ascontiguousarray(np.asarray(i["L0_attn_b_out"]).reshape(8, 128).T)
    d["nw0"] = np.ascontiguousarray(np.asarray(i["L0_ffn_norm_w"]).reshape(8, 128).T)
    d["gate_w0"] = np.ascontiguousarray(np.asarray(i["L0_ffn_gate_w"]).T)
    d["gate_b0"] = np.ascontiguousarray(np.asarray(i["L0_ffn_gate_b"]).reshape(1, 8))
    w1t = np.transpose(i["L0_ffn_mlp1_w"], (0, 2, 1))
    d["w1t0"] = np.ascontiguousarray(np.concatenate([w1t[:, :, 0::2], w1t[:, :, 1::2]], axis=2))
    b1 = np.asarray(i["L0_ffn_mlp1_b"])
    b1r = np.concatenate([b1[:, 0::2], b1[:, 1::2]], axis=1)
    d["b1t0"] = np.ascontiguousarray(np.transpose(b1r.reshape(8, 16, 128), (0, 2, 1)))
    d["w2t0"] = np.ascontiguousarray(np.transpose(i["L0_ffn_mlp2_w"], (0, 2, 1)))
    d["b2_0"] = np.ascontiguousarray(np.asarray(i["L0_ffn_mlp2_b"]))

    Wq = np.asarray(i["L1_attn_W_qkv"])
    bq = np.asarray(i["L1_attn_b_qkv"])
    wqk = np.zeros((D, 16 * 128), dtype=f32)
    wv = np.zeros((D, 16 * 64), dtype=f32)
    bqk = np.zeros((128, 16), dtype=f32)
    bv = np.zeros((128, 8), dtype=f32)
    for r in range(16):
        wqk[:, r * 128:r * 128 + 64] = Wq[r * 192:r * 192 + 64, :].T
        wqk[:, r * 128 + 64:r * 128 + 128] = Wq[r * 192 + 64:r * 192 + 128, :].T
        wv[:, r * 64:(r + 1) * 64] = Wq[r * 192 + 128:r * 192 + 192, :].T
        bqk[0:64, r] = bq[r * 192:r * 192 + 64]
        bqk[64:128, r] = bq[r * 192 + 64:r * 192 + 128]
        bv[(r % 2) * 64:(r % 2) * 64 + 64, r // 2] = bq[r * 192 + 128:r * 192 + 192]
    d["wqk1"], d["wv1"], d["bqk1"], d["bv1"] = wqk, wv, bqk, bv
    d["w_out1"] = np.ascontiguousarray(np.asarray(i["L1_attn_W_out"]).T)
    d["b_out1"] = np.ascontiguousarray(np.asarray(i["L1_attn_b_out"]).reshape(8, 128).T)
    d["wg1"] = np.ascontiguousarray(np.asarray(i["L1_ffn_Wg"]))
    d["w1_1"] = np.ascontiguousarray(np.asarray(i["L1_ffn_W1"]))
    d["w2_1"] = np.ascontiguousarray(np.asarray(i["L1_ffn_W2"]))

    d["ones_col"] = np.ones((128, 1), dtype=f32)
    d["ones_row"] = np.ones((1, 128), dtype=f32)
    d["ident_bf"] = np.eye(128, dtype=BF)
    d["ones_bf"] = np.ones((128, 128), dtype=BF)
    invf = 10000.0 ** (-np.arange(0, 32, dtype=np.float64) / 32.0)
    fr = np.outer(np.arange(S, dtype=np.float64), invf)
    cs = np.cos(fr).astype(f32).reshape(32, 128, 32)
    sn = np.sin(fr).astype(f32).reshape(32, 128, 32)
    d["cos_tbl"] = np.ascontiguousarray(np.transpose(cs, (1, 0, 2)).reshape(128, 1024))
    d["sin_tbl"] = np.ascontiguousarray(np.transpose(sn, (1, 0, 2)).reshape(128, 1024))
    sl = _slopes16() * MAX_BIAS
    bt = np.zeros((8, 128, 512), dtype=f32)
    for h in range(16):
        for kt in range(32):
            for qb in range(8):
                bt[h // 2, :, (h % 2) * 256 + kt * 8 + qb] = (
                    sl[h] * (kt * 128 + np.arange(128) - 512.0 * qb))
    d["bias_tbl0"] = bt
    d["iota_loc"] = np.arange(TG, dtype=f32).reshape(1, TG)
    p_ = np.arange(128)[:, None]
    f_ = np.arange(TG)[None, :]
    d["masks"] = np.stack(
        [(f_ >= p_ + r * 128).astype(f32) for r in range(4)]).astype(BF)
    d["masksneg"] = np.ascontiguousarray(np.concatenate(
        [np.where(f_ >= p_ + r * 128, 0.0, -1.0e38).astype(f32) for r in range(4)],
        axis=1))
    sel = np.zeros((8, 8 * 128), dtype=BF)
    for e in range(8):
        sel[e, e * 128:(e + 1) * 128] = 1
    d["sel8"] = sel
    return d



def emit_consts(p, ctx, hbm, need):
    nc = p.nc
    E = ctx.enter_context
    shapes = {
        "ones_col": ([128, 1], F32), "ones_row": ([1, 128], F32),
        "ident_bf": ([128, 128], BF16), "ones_bf": ([128, 128], BF16),
        "cos_tbl": ([128, 1024], F32), "sin_tbl": ([128, 1024], F32),
        "masks": ([128, 4 * TG], BF16),
        "masksneg": ([128, 4 * TG], F32),
        "sel8": ([8, 8 * 128], BF16),
        "nw0": ([128, 8], F32), "b_out0": ([128, 8], F32),
        "gate_b0": ([1, 8], F32),
        "bqk1": ([128, 16], F32), "bv1": ([128, 8], F32), "b_out1": ([128, 8], F32),
    }
    cst = NS()
    for name in need:
        shp, dt = shapes[name]
        buf = E(nc.sbuf_tensor("c_" + name, shp, dt))
        setattr(cst, name, buf)
        src = hbm[name]
        if name == "masks":
            src = src.rearrange("r p t -> p (r t)")
        p.dma("sync", lambda e, b=buf, s=src: e.dma_start(out=b[:], in_=s), writes=[buf])
    return cst



def emit_ln(p, ctx, io, cst, block0):
    nc = p.nc
    E = ctx.enter_context
    x_col = [E(nc.sbuf_tensor(f"ln_x{i}", [128, DC * TG], F32)) for i in range(2)]
    xn_col = E(nc.sbuf_tensor("ln_xn", [128, DC * TG], F32))
    sqx = E(nc.sbuf_tensor("ln_sqx", [128, TG], F32))
    hsc = E(nc.sbuf_tensor("ln_hsc", [128, TG], F32))
    out_f = [E(nc.sbuf_tensor(f"ln_of{i}", [128, DC * TG], F32)) for i in range(2)]
    out_h = [E(nc.sbuf_tensor(f"ln_oh{i}", [128, DC * TG], BF16)) for i in range(2)]
    out_xn = [E(nc.sbuf_tensor(f"ln_oxn{i}", [128, DC * TG], BF16)) for i in range(2)]
    rows = E(nc.sbuf_tensor("ln_rows", [1, 4 * TG], F32))
    ps_sum = E(nc.psum_tensor("ln_ps_sum", [1, TG], F32))
    ps_sq = E(nc.psum_tensor("ln_ps_sq", [1, TG], F32))
    ps_mbc = E(nc.psum_tensor("ln_ps_mbc", [128, TG], F32))
    ps_rbc = E(nc.psum_tensor("ln_ps_rbc", [128, TG], F32))
    ps_rms = E(nc.psum_tensor("ln_ps_rms", [128, TG], F32))

    def row(k):
        return rows[:, k * TG:(k + 1) * TG]

    R_M, R_SD, R_T, R_RMS = 0, 1, 2, 3

    for tg in range(NTG):
        xb = x_col[tg % 2]
        p.dma("sync", lambda e, xb=xb, tg=tg: e.dma_start(out=xb[:], in_=col_ap(io.x_in, tg)),
              reads=[io.x_in], writes=[xb])
        for dc in range(DC):
            p.op("tensor", lambda e, xb=xb, dc=dc: e.matmul(
                ps_sum[:], cst.ones_col[:], xb[:, dc * TG:(dc + 1) * TG],
                start=(dc == 0), stop=(dc == DC - 1)),
                reads=[xb, cst.ones_col], writes=[ps_sum])
        for dc in range(DC):
            p.op("scalar", lambda e, xb=xb, dc=dc: e.activation(
                sqx[:], xb[:, dc * TG:(dc + 1) * TG], AF.Square),
                reads=[xb], writes=[sqx])
            p.op("tensor", lambda e, dc=dc: e.matmul(
                ps_sq[:], cst.ones_col[:], sqx[:], start=(dc == 0), stop=(dc == DC - 1)),
                reads=[sqx, cst.ones_col], writes=[ps_sq])
        p.op("vector", lambda e: e.tensor_scalar(row(R_M), ps_sum[:], 1.0 / D, None, ALU.mult),
             reads=[ps_sum], writes=["r_m"])
        p.op("vector", lambda e: e.tensor_scalar(row(R_T), ps_sq[:], 1.0 / D, None, ALU.mult),
             reads=[ps_sq], writes=["r_t"])
        p.op("vector", lambda e: e.tensor_tensor(row(R_SD), row(R_M), row(R_M), ALU.mult),
             reads=["r_m"], writes=["r_sd"])
        p.op("vector", lambda e: e.tensor_tensor(row(R_T), row(R_T), row(R_SD), ALU.subtract),
             reads=["r_t", "r_sd"], writes=["r_t"])
        p.op("vector", lambda e: e.tensor_scalar(row(R_T), row(R_T), LN_EPS, None, ALU.add),
             reads=["r_t"], writes=["r_t"])
        p.op("scalar", lambda e: e.activation(row(R_SD), row(R_T), AF.Sqrt),
             reads=["r_t"], writes=["r_sd"])
        p.op("vector", lambda e: e.reciprocal(row(R_SD), row(R_SD)),
             reads=["r_sd"], writes=["r_sd"])
        p.op("tensor", lambda e: e.matmul(ps_mbc[:], cst.ones_row[:], row(R_M),
                                          start=True, stop=True),
             reads=["r_m", cst.ones_row], writes=[ps_mbc])
        p.op("tensor", lambda e: e.matmul(ps_rbc[:], cst.ones_row[:], row(R_SD),
                                          start=True, stop=True),
             reads=["r_sd", cst.ones_row], writes=[ps_rbc])
        for dc in range(DC):
            c = slice(dc * TG, (dc + 1) * TG)
            p.op("vector", lambda e, xb=xb, c=c: e.tensor_tensor(
                xn_col[:, c], xb[:, c], ps_mbc[:], ALU.subtract),
                reads=[xb, ps_mbc], writes=[(xn_col)])
            p.op("vector", lambda e, c=c: e.tensor_tensor(
                xn_col[:, c], xn_col[:, c], ps_rbc[:], ALU.mult),
                reads=[xn_col, ps_rbc], writes=[xn_col])
            if block0:
                p.op("scalar", lambda e, c=c: e.activation(sqx[:], xn_col[:, c], AF.Square),
                     reads=[xn_col], writes=[sqx])
                p.op("tensor", lambda e, dc=dc: e.matmul(
                    ps_rms[0:1, :], cst.ones_col[:], sqx[:],
                    start=(dc == 0), stop=(dc == DC - 1)),
                    reads=[sqx, cst.ones_col], writes=["ps_rms_row"])
        if block0:
            p.op("vector", lambda e: e.tensor_scalar(row(R_RMS), ps_rms[0:1, :], 1.0 / D,
                                                     None, ALU.mult),
                 reads=["ps_rms_row"], writes=["r_rms"])
            p.op("vector", lambda e: e.tensor_scalar(row(R_RMS), row(R_RMS), RMS_EPS,
                                                     None, ALU.add),
                 reads=["r_rms"], writes=["r_rms"])
            p.op("scalar", lambda e: e.activation(row(R_RMS), row(R_RMS), AF.Sqrt),
                 reads=["r_rms"], writes=["r_rms"])
            p.op("vector", lambda e: e.reciprocal(row(R_RMS), row(R_RMS)),
                 reads=["r_rms"], writes=["r_rms"])
            p.op("tensor", lambda e: e.matmul(ps_rms[:], cst.ones_row[:], row(R_RMS),
                                              start=True, stop=True),
                 reads=["r_rms", "ps_rms_row", cst.ones_row], writes=[ps_rms])
        of, oh, oxn = out_f[tg % 2], out_h[tg % 2], out_xn[tg % 2]
        for dc in range(DC):
            c = slice(dc * TG, (dc + 1) * TG)
            p.op("vector", lambda e, c=c, oxn=oxn: e.tensor_copy(oxn[:, c], xn_col[:, c]),
                 reads=[xn_col], writes=[oxn])
            if block0:
                p.op("vector", lambda e, xb=xb, c=c, of=of: e.tensor_tensor(
                    of[:, c], xb[:, c], xn_col[:, c], ALU.add),
                    reads=[xb, xn_col], writes=[of])
                p.op("vector", lambda e, c=c, dc=dc: e.tensor_scalar(
                    hsc[:], xn_col[:, c], cst.nw0[:, dc:dc + 1], None, ALU.mult),
                    reads=[xn_col, cst.nw0], writes=[hsc])
                p.op("vector", lambda e, c=c, oh=oh: e.tensor_tensor(
                    oh[:, c], hsc[:], ps_rms[:], ALU.mult),
                    reads=[hsc, ps_rms], writes=[oh])
        p.dma("sync", lambda e, oxn=oxn, tg=tg: e.dma_start(out=col_ap(io.xn, tg), in_=oxn[:]),
              reads=[oxn], writes=[io.xn])
        if block0:
            p.dma("scalar", lambda e, of=of, tg=tg: e.dma_start(out=col_ap(io.xacc, tg), in_=of[:]),
                  reads=[of], writes=[io.xacc])
            p.dma("gpsimd", lambda e, oh=oh, tg=tg: e.dma_start(out=col_ap(io.h, tg), in_=oh[:]),
                  reads=[oh], writes=[io.h])





def emit_attn0(p, ctx, io, cst):
    nc = p.nc
    E = ctx.enter_context
    ctx_all = [E(nc.sbuf_tensor(f"a0_ctx{i}", [128, S], BF16)) for i in range(8)]
    with ExitStack() as hctx:
        EH = hctx.enter_context
        xng = [EH(nc.sbuf_tensor(f"a0_xng{i}", [128, DC * TG], BF16)) for i in range(2)]
        wq_sb = [EH(nc.sbuf_tensor(f"a0_wq{i}", [128, DC * 384], BF16)) for i in range(2)]
        bt_sb = [EH(nc.sbuf_tensor(f"a0_bt{i}", [128, 512], F32)) for i in range(2)]
        bq_bf = EH(nc.sbuf_tensor("a0_bq", [1, NH * 192], BF16))
        qkv_tm = EH(nc.sbuf_tensor("a0_qkvtm", [128, 4 * 384], F32))
        rsc = EH(nc.sbuf_tensor("a0_rsc", [128, 4 * 256], F32))
        qk_rot = EH(nc.sbuf_tensor("a0_qkrot", [128, 4 * 256], BF16))
        qf = [EH(nc.sbuf_tensor(f"a0_qf{i}", [65, S], BF16)) for i in range(2)]
        kf = [EH(nc.sbuf_tensor(f"a0_kf{i}", [65, S], BF16)) for i in range(2)]
        va = [EH(nc.sbuf_tensor(f"a0_va{i}", [128, 32 * 65], BF16)) for i in range(2)]
        qsq = EH(nc.sbuf_tensor("a0_qsq", [64, TG], BF16))
        u65 = EH(nc.sbuf_tensor("a0_u65", [65, TG], F32))
        iot = EH(nc.sbuf_tensor("a0_iot", [65, TG], F32))
        krow = EH(nc.sbuf_tensor("a0_krow", [65, 20], F32))
        e_sb = [EH(nc.sbuf_tensor(f"a0_e{i}", [128, TG], BF16)) for i in range(2)]
        rz = EH(nc.sbuf_tensor("a0_rz", [65, TG], BF16))
        rzf = EH(nc.sbuf_tensor("a0_rzf", [65, TG], F32))
        cx_sb = EH(nc.sbuf_tensor("a0_cx", [128, TG], BF16))
        cxb = EH(nc.sbuf_tensor("a0_cxb", [64, TG], BF16))
        ps_qkv = EH(nc.psum_tensor("a0_ps_qkv", [128, 384], F32))
        ps_tp = [EH(nc.psum_tensor(f"a0_ps_tp{i}", [64, TG], BF16)) for i in range(2)]
        ps_st = EH(nc.psum_tensor("a0_ps_st", [128, TG], F32))
        ps_s = [EH(nc.psum_tensor(f"a0_ps_s{i}", [128, TG], F32)) for i in range(2)]
        ps_ctx = EH(nc.psum_tensor("a0_ps_ctx", [65, TG], F32))
        ps_rzb = EH(nc.psum_tensor("a0_ps_rzb", [64, TG], F32))

        p.dma("gpsimd", lambda e: e.dma_start(out=bq_bf[:], in_=io.hbm["b_qkv0"]),
              reads=[io.hbm["b_qkv0"]], writes=[bq_bf])
        p.dma("sync", lambda e: e.dma_start(out=iot[64:65, :], in_=io.hbm["iota_loc"]),
              reads=[io.hbm["iota_loc"]], writes=[iot])
        for i in range(2):
            p.op("vector", lambda e, i=i: e.memset(
                va[i][:].rearrange("p (t c) -> p t c", c=65)[:, :, 64:65], 1.0),
                writes=[va[i]])
            p.op("vector", lambda e, i=i: e.memset(kf[i][64:65, :], 1.0), writes=[kf[i]])

        slopes = (_slopes16() * MAX_BIAS).tolist()

        for hp in range(8):
            wq = wq_sb[hp % 2]
            bt = bt_sb[hp % 2]
            p.dma("gpsimd", lambda e, wq=wq, hp=hp: e.dma_start(
                out=wq[:].rearrange("p (c f) -> p c f", c=DC),
                in_=io.hbm["w_qkv0"].rearrange("(c p) f -> p c f", p=128)[:, :, hp * 384:(hp + 1) * 384]),
                reads=[io.hbm["w_qkv0"]], writes=[wq])
            p.dma("gpsimd", lambda e, bt=bt, hp=hp: e.dma_start(
                out=bt[:], in_=io.hbm["bias_tbl0"][hp]),
                reads=[io.hbm["bias_tbl0"]], writes=[bt])
            for g in range(8):
                xg = xng[g % 2]
                p.dma("sync", lambda e, xg=xg, g=g: e.dma_start(
                    out=xg[:], in_=col_ap(io.xn, g)),
                    reads=[io.xn], writes=[xg])
                for t4 in range(4):
                    for dc in range(DC):
                        p.op("tensor", lambda e, t4=t4, dc=dc, wq=wq, xg=xg: e.matmul(
                            ps_qkv[:], xg[:, dc * TG + t4 * 128:dc * TG + (t4 + 1) * 128],
                            wq[:, dc * 384:(dc + 1) * 384], start=(dc == 0), stop=False),
                            reads=[xg, wq], writes=[ps_qkv])
                    p.op("tensor", lambda e, hp=hp: e.matmul(
                        ps_qkv[:], cst.ones_bf[0:1, 0:128], bq_bf[:, hp * 384:(hp + 1) * 384],
                        start=False, stop=True),
                        reads=[cst.ones_bf, bq_bf], writes=[ps_qkv])
                    p.op("vector", lambda e, t4=t4: e.tensor_copy(
                        qkv_tm[:, t4 * 384:(t4 + 1) * 384], ps_qkv[:]),
                        reads=[ps_qkv], writes=[qkv_tm])
                q4 = qkv_tm[:].rearrange("p (t h x) -> p t h x", t=4, h=2)
                cosg = cst.cos_tbl[:, g * 128:(g + 1) * 128].rearrange(
                    "p (t j) -> p t j", j=32).unsqueeze(2).broadcast_to([128, 4, 2, 32])
                sing = cst.sin_tbl[:, g * 128:(g + 1) * 128].rearrange(
                    "p (t j) -> p t j", j=32).unsqueeze(2).broadcast_to([128, 4, 2, 32])
                rs = rsc[:].rearrange("p (k t h j) -> p k t h j", k=4, t=4, j=32)
                rot6 = qk_rot[:].rearrange("p (t a h b j) -> p t a h b j", t=4, a=2, h=2, b=2)
                for a, off in ((0, 0), (1, 64)):
                    x1 = q4[:, :, :, off + 0:off + 32]
                    x2 = q4[:, :, :, off + 32:off + 64]
                    eng = "vector" if a == 0 else "gpsimd"
                    p.op(eng, lambda e, x1=x1, c=cosg: e.tensor_tensor(rs[:, 0], x1, c, ALU.mult),
                         reads=[qkv_tm, cst.cos_tbl], writes=["a0rs0"])
                    p.op(eng, lambda e, x2=x2, s=sing: e.tensor_tensor(rs[:, 1], x2, s, ALU.mult),
                         reads=[qkv_tm, cst.sin_tbl], writes=["a0rs1"])
                    p.op(eng, lambda e, a=a: e.tensor_tensor(
                        rot6[:, :, a, :, 0, :], rs[:, 0], rs[:, 1], ALU.subtract),
                        reads=["a0rs0", "a0rs1"], writes=[qk_rot])
                    p.op(eng, lambda e, x2=x2, c=cosg: e.tensor_tensor(rs[:, 2], x2, c, ALU.mult),
                         reads=[qkv_tm, cst.cos_tbl], writes=["a0rs2"])
                    p.op(eng, lambda e, x1=x1, s=sing: e.tensor_tensor(rs[:, 3], x1, s, ALU.mult),
                         reads=[qkv_tm, cst.sin_tbl], writes=["a0rs3"])
                    p.op(eng, lambda e, a=a: e.tensor_tensor(
                        rot6[:, :, a, :, 1, :], rs[:, 2], rs[:, 3], ALU.add),
                        reads=["a0rs2", "a0rs3"], writes=[qk_rot])
                for h2 in range(2):
                    p.op("vector", lambda e, h2=h2, g=g: e.tensor_copy(
                        va[h2][:, g * 4 * 65:(g + 1) * 4 * 65].rearrange(
                            "p (t c) -> p t c", c=65)[:, :, 0:64],
                        q4[:, :, h2, 128:192]),
                        reads=[qkv_tm], writes=[va[h2]])
                for si, (a, h2) in enumerate(((0, 0), (0, 1), (1, 0), (1, 1))):
                    tp = ps_tp[si % 2]
                    for t4 in range(4):
                        col = t4 * 256 + a * 128 + h2 * 64
                        p.op("tensor", lambda e, tp=tp, t4=t4, col=col: e.transpose(
                            tp[0:64, t4 * 128:(t4 + 1) * 128],
                            qk_rot[:, col:col + 64], cst.ident_bf[:]),
                            reads=[qk_rot, cst.ident_bf], writes=[tp])
                    dst = qf[h2] if a == 0 else kf[h2]
                    p.op("vector", lambda e, tp=tp, dst=dst, g=g: e.tensor_copy(
                        dst[0:64, g * TG:(g + 1) * TG], tp[0:64, :]),
                        reads=[tp], writes=[dst])
            for h2 in range(2):
                h = 2 * hp + h2
                for c in range(8):
                    p.op("scalar", lambda e, c=c, h2=h2: e.activation(
                        qsq[:], kf[h2][0:64, c * TG:(c + 1) * TG], AF.Square),
                        reads=[kf[h2]], writes=[qsq])
                    p.op("tensor", lambda e: e.matmul(
                        ps_st[64:65, :], cst.ones_bf[0:64, 0:1], qsq[:],
                        start=True, stop=True, tile_position=(0, 64)),
                        reads=[qsq, cst.ones_bf], writes=[ps_st])
                    p.op("vector", lambda e, c=c: e.reduce_max(
                        krow[64:65, c:c + 1], ps_st[64:65, :], AX.X),
                        reads=[ps_st], writes=[krow])
                p.op("vector", lambda e: e.reduce_max(
                    krow[64:65, 8:9], krow[64:65, 0:8], AX.X), reads=[krow], writes=["a0km"])
                p.op("scalar", lambda e: e.activation(
                    krow[64:65, 9:10], krow[64:65, 8:9], AF.Sqrt),
                    reads=["a0km"], writes=["a0km2"])
                for c in range(8):
                    p.op("scalar", lambda e, c=c, h2=h2: e.activation(
                        qsq[:], qf[h2][0:64, c * TG:(c + 1) * TG], AF.Square),
                        reads=[qf[h2]], writes=[qsq])
                    p.op("tensor", lambda e: e.matmul(
                        ps_st[64:65, :], cst.ones_bf[0:64, 0:1], qsq[:],
                        start=True, stop=True, tile_position=(0, 64)),
                        reads=[qsq, cst.ones_bf], writes=[ps_st])
                    p.op("vector", lambda e, c=c: e.reduce_max(
                        krow[64:65, 10 + c:11 + c], ps_st[64:65, :], AX.X),
                        reads=[ps_st], writes=["a0qm"])
                p.op("vector", lambda e: e.reduce_max(
                    krow[64:65, 2:3], krow[64:65, 10:14], AX.X), reads=["a0qm"], writes=["a0qx"])
                p.op("vector", lambda e: e.reduce_max(
                    krow[64:65, 3:4], krow[64:65, 14:18], AX.X), reads=["a0qm"], writes=["a0qy"])
                p.op("vector", lambda e: e.tensor_tensor(
                    krow[64:65, 4:5], krow[64:65, 2:3], krow[64:65, 3:4], ALU.max),
                    reads=["a0qx", "a0qy"], writes=["a0qz"])
                p.op("scalar", lambda e: e.activation(
                    krow[64:65, 5:6], krow[64:65, 4:5], AF.Sqrt),
                    reads=["a0qz"], writes=["a0qs"])
                p.op("vector", lambda e: e.tensor_tensor(
                    krow[64:65, 6:7], krow[64:65, 5:6], krow[64:65, 9:10], ALU.mult),
                    reads=["a0qs", "a0km2"], writes=["a0bnd"])
                for qb in range(8):
                    p.op("vector", lambda e, h=h, h2=h2, qb=qb: e.scalar_tensor_tensor(
                        qf[h2][64:65, qb * TG:(qb + 1) * TG], iot[64:65, :],
                        -SQ192 * slopes[h], krow[64:65, 6:7].to_broadcast([1, TG]),
                        ALU.mult, ALU.subtract),
                        reads=[iot, "a0bnd"], writes=[qf[h2]])
                for qb in range(8):
                    nkt = min(4 * qb + 4, 32)
                    kt0 = max(0, int((qb * 512 - 120.0 / slopes[h] - 127) // 128))
                    for k in range(kt0, nkt + 1):
                        if k < nkt:
                            kt = k
                            sb = ps_s[kt % 2]
                            eb = e_sb[kt % 2]
                            p.op("tensor", lambda e, sb=sb, h2=h2, kt=kt, qb=qb: e.matmul(
                                sb[:], kf[h2][0:65, kt * 128:(kt + 1) * 128],
                                qf[h2][0:65, qb * TG:(qb + 1) * TG], start=True, stop=True),
                                reads=[kf[h2], qf[h2]], writes=[sb])
                            rel = kt - 4 * qb
                            if 0 <= rel <= 3:
                                p.op("vector", lambda e, sb=sb, rel=rel: e.tensor_tensor(
                                    sb[:], sb[:], cst.masksneg[:, rel * TG:(rel + 1) * TG], ALU.add),
                                    reads=[sb, cst.masksneg], writes=[sb])
                            p.op("scalar", lambda e, sb=sb, eb=eb, h2=h2, kt=kt, qb=qb, bt=bt:
                                 e.activation(eb[:], sb[:], AF.Exp, scale=ISQ192,
                                              bias=bt[:, h2 * 256 + kt * 8 + qb:h2 * 256 + kt * 8 + qb + 1]),
                                 reads=[sb, bt], writes=[eb])
                        if k > kt0:
                            kt = k - 1
                            eb = e_sb[kt % 2]
                            p.op("tensor", lambda e, eb=eb, h2=h2, kt=kt, first=(kt == kt0),
                                 last=(kt == nkt - 1):
                                 e.matmul(ps_ctx[:], va[h2][:, kt * 65:(kt + 1) * 65], eb[:],
                                          start=first, stop=last),
                                 reads=[eb, va[h2]], writes=[ps_ctx])
                    p.op("vector", lambda e: e.reciprocal(rzf[64:65, :], ps_ctx[64:65, :]),
                         reads=[ps_ctx], writes=[rzf])
                    p.op("vector", lambda e: e.tensor_copy(rz[64:65, :], rzf[64:65, :]),
                         reads=[rzf], writes=[rz])
                    p.op("tensor", lambda e: e.matmul(
                        ps_rzb[:], cst.ones_bf[64:65, 0:64], rz[64:65, :],
                        start=True, stop=True),
                        reads=[rz, cst.ones_bf], writes=[ps_rzb])
                    p.op("vector", lambda e: e.tensor_copy(cx_sb[0:64, :], ps_ctx[0:64, :]),
                         reads=[ps_ctx], writes=[cx_sb])
                    if h2 == 0:
                        p.op("vector", lambda e, hp=hp, qb=qb: e.tensor_tensor(
                            ctx_all[hp][0:64, qb * TG:(qb + 1) * TG],
                            cx_sb[0:64, :], ps_rzb[:], ALU.mult),
                            reads=[cx_sb, ps_rzb], writes=[ctx_all[hp]])
                    else:
                        p.op("vector", lambda e: e.tensor_tensor(
                            cxb[0:64, :], cx_sb[0:64, :], ps_rzb[:], ALU.mult),
                            reads=[cx_sb, ps_rzb], writes=[cxb])
                        p.dma("sync", lambda e, hp=hp, qb=qb: e.dma_start(
                            out=ctx_all[hp][64:128, qb * TG:(qb + 1) * TG], in_=cxb[0:64, :]),
                            reads=[cxb], writes=[ctx_all[hp]])
    with ExitStack() as octx:
        EO = octx.enter_context
        wo_sb = EO(nc.sbuf_tensor("a0_wo", [128, 8 * 1024], BF16))
        ot = [EO(nc.sbuf_tensor(f"a0_ot{i}", [128, TG], F32)) for i in range(2)]
        ps_o = [EO(nc.psum_tensor(f"a0_ps_o{i}", [128, TG], F32)) for i in range(2)]
        p.dma("gpsimd", lambda e: e.dma_start(
            out=wo_sb[:].rearrange("p (c f) -> p c f", c=8),
            in_=io.hbm["w_out0"].rearrange("(c p) f -> p c f", p=128)),
            reads=[io.hbm["w_out0"]], writes=[wo_sb])
        for mt in range(8):
            for tg in range(8):
                sl = (mt * 8 + tg) % 2
                for ec in range(8):
                    p.op("tensor", lambda e, sl=sl, ec=ec, mt=mt, tg=tg: e.matmul(
                        ps_o[sl][:], wo_sb[:, ec * 1024 + mt * 128:ec * 1024 + (mt + 1) * 128],
                        ctx_all[ec][:, tg * TG:(tg + 1) * TG],
                        start=(ec == 0), stop=(ec == 7)),
                        reads=[wo_sb, ctx_all[ec]], writes=[ps_o[sl]])
                p.op("vector", lambda e, sl=sl, mt=mt: e.tensor_scalar(
                    ot[sl][:], ps_o[sl][:], cst.b_out0[:, mt:mt + 1], None, ALU.add),
                    reads=[ps_o[sl], cst.b_out0], writes=[ot[sl]])
                p.dma("sync", lambda e, sl=sl, mt=mt, tg=tg: e.dma_start(
                    out=io.attnout.rearrange("(c p) t -> p c t", p=128)[:, mt, tg * TG:(tg + 1) * TG],
                    in_=ot[sl][:]),
                    reads=[ot[sl]], writes=[io.attnout])



def emit_ffn(p, ctx, io, cst, block0):
    nc = p.nc
    E = ctx.enter_context
    cT = E(nc.sbuf_tensor("f_cT", [8, S], BF16))
    macc = E(nc.sbuf_tensor("f_macc", [128, DC * 2048], F32))
    gw_sb = E(nc.sbuf_tensor("f_gw", [128, DC * 8], BF16))
    lgt = E(nc.sbuf_tensor("f_lgt", [128, 16 * 8], F32))
    cbf = E(nc.sbuf_tensor("f_cbf", [128, 8], BF16))
    h_roll = [E(nc.sbuf_tensor(f"f_h{i}", [128, DC * TG], BF16)) for i in range(2)]
    if block0:
        gb_bf = E(nc.sbuf_tensor("f_gb", [1, 8], BF16))
        b2_sb = E(nc.sbuf_tensor("f_b2", [8, 1024], BF16))
        b1t_sb = E(nc.sbuf_tensor("f_b1t", [128, 16], F32))
        gsb = E(nc.sbuf_tensor("f_gsb", [128, TG], F32))
        sig = E(nc.sbuf_tensor("f_sig", [128, TG], BF16))
        gcb = E(nc.sbuf_tensor("f_gcb", [128, TG], BF16))
        lsb = E(nc.sbuf_tensor("f_lsb", [128, TG], F32))
        lbf = E(nc.sbuf_tensor("f_lbf", [128, TG], BF16))
        gs2 = E(nc.sbuf_tensor("f_gs2", [128, TG], BF16))
    else:
        scbc = E(nc.sbuf_tensor("f_scbc", [128, TG], BF16))
        y1 = E(nc.sbuf_tensor("f_y1", [128, TG], BF16))
    nact = 8 if block0 else 16
    act = E(nc.sbuf_tensor("f_act", [128, nact * TG], BF16))
    w1_sb = [E(nc.sbuf_tensor(f"f_w1{i}", [128, DC * 2048], BF16)) for i in range(2)]
    w2cols = 8 * 1024 if block0 else 16 * 1024
    w2_sb = E(nc.sbuf_tensor("f_w2", [128, w2cols], BF16))
    att_view = h_roll[0][:].bitcast(F32)
    att_roll = [att_view, att_view]

    gw_src = "gate_w0" if block0 else "wg1"
    rhs_hbm = io.h
    p.dma("gpsimd", lambda e: e.dma_start(
        out=gw_sb[:].rearrange("p (c f) -> p c f", c=DC),
        in_=io.hbm[gw_src].rearrange("(c p) f -> p c f", p=128)),
        reads=[io.hbm[gw_src]], writes=[gw_sb])
    if block0:
        p.dma("gpsimd", lambda e: e.dma_start(out=gb_bf[:], in_=io.hbm["gate_b0"]),
              reads=[io.hbm["gate_b0"]], writes=[gb_bf])
        p.dma("gpsimd", lambda e: e.dma_start(out=b2_sb[:], in_=io.hbm["b2_0"]),
              reads=[io.hbm["b2_0"]], writes=[b2_sb])

    def gate_tile(tg, q, sl, ps_g, ps_ct):
        hb = h_roll[sl]
        for dc in range(DC):
            p.op("tensor", lambda e, dc=dc, q=q, hb=hb: e.matmul(
                ps_g[:], hb[:, dc * TG + q * 128:dc * TG + (q + 1) * 128],
                gw_sb[:, dc * 8:(dc + 1) * 8], start=(dc == 0), stop=not block0 and dc == DC - 1),
                reads=[hb, gw_sb], writes=[ps_g])
        if block0:
            p.op("tensor", lambda e: e.matmul(
                ps_g[:], cst.ones_bf[0:1, 0:128], gb_bf[:], start=False, stop=True),
                reads=[cst.ones_bf, gb_bf], writes=[ps_g])
        L = lambda k: lgt[:, k * 8:(k + 1) * 8]
        M = lambda k: lgt[:, 64 + k]
        p.op("vector", lambda e: e.tensor_copy(L(0), ps_g[:]), reads=[ps_g], writes=["fL0"])
        p.op("vector", lambda e: e.reduce_max(lgt[:, 64:65], L(0), AX.X),
             reads=["fL0"], writes=["fm1"])
        p.op("vector", lambda e: e.tensor_tensor(
            L(1), L(0), lgt[:, 64:65].to_broadcast([128, 8]), ALU.is_ge),
             reads=["fL0", "fm1"], writes=["fL1"])
        p.op("vector", lambda e: e.scalar_tensor_tensor(L(2), L(1), -1e30, L(0),
                                                        ALU.mult, ALU.add),
             reads=["fL1", "fL0"], writes=["fL2"])
        p.op("vector", lambda e: e.reduce_max(lgt[:, 65:66], L(2), AX.X),
             reads=["fL2"], writes=["fm2"])
        p.op("vector", lambda e: e.tensor_tensor(
            L(3), L(2), lgt[:, 65:66].to_broadcast([128, 8]), ALU.is_ge),
             reads=["fL2", "fm2"], writes=["fL3"])
        if block0:
            p.op("vector", lambda e: e.tensor_tensor(lgt[:, 66:67], lgt[:, 64:65],
                                                     lgt[:, 65:66], ALU.subtract),
                 reads=["fm1", "fm2"], writes=["fw1"])
            p.op("scalar", lambda e: e.activation(lgt[:, 66:67], lgt[:, 66:67], AF.Sigmoid),
                 reads=["fw1"], writes=["fw1"])
            p.op("vector", lambda e: e.tensor_scalar(lgt[:, 67:68], lgt[:, 66:67], -1.0, 1.0,
                                                     ALU.mult, ALU.add),
                 reads=["fw1"], writes=["fw2"])
            p.op("vector", lambda e: e.tensor_tensor(
                L(4), L(1), lgt[:, 66:67].to_broadcast([128, 8]), ALU.mult),
                 reads=["fL1", "fw1"], writes=["fL4"])
            p.op("vector", lambda e: e.tensor_tensor(
                L(5), L(3), lgt[:, 67:68].to_broadcast([128, 8]), ALU.mult),
                 reads=["fL3", "fw2"], writes=["fL5"])
            p.op("vector", lambda e: e.tensor_tensor(cbf[:], L(5), L(4), ALU.add),
                 reads=["fL5", "fL4"], writes=[cbf])
        else:
            p.op("vector", lambda e: e.tensor_scalar(lgt[:, 67:68], lgt[:, 64:65], -1.0,
                                                     None, ALU.mult),
                 reads=["fm1"], writes=["fneg"])
            p.op("scalar", lambda e: e.activation(L(4), L(0), AF.Exp, bias=lgt[:, 67:68],
                                                  accum_out=lgt[:, 68:69]),
                 reads=["fL0", "fneg"], writes=["fL4", "fZ"])
            p.op("vector", lambda e: e.tensor_tensor(L(3), L(3), L(1), ALU.add),
                 reads=["fL3", "fL1"], writes=["fL3"])
            p.op("vector", lambda e: e.tensor_tensor(L(5), L(4), L(3), ALU.mult),
                 reads=["fL4", "fL3"], writes=["fL5"])
            p.op("vector", lambda e: e.tensor_reduce(lgt[:, 66:67], L(5), AX.X, ALU.add),
                 reads=["fL5"], writes=["fden"])
            p.op("vector", lambda e: e.scalar_tensor_tensor(
                lgt[:, 66:67], lgt[:, 68:69], 1e-8, lgt[:, 66:67], ALU.mult, ALU.add),
                reads=["fZ", "fden"], writes=["fden"])
            p.op("vector", lambda e: e.reciprocal(lgt[:, 67:68], lgt[:, 66:67]),
                 reads=["fden"], writes=["frec"])
            p.op("vector", lambda e: e.tensor_tensor(
                L(6), L(5), lgt[:, 67:68].to_broadcast([128, 8]), ALU.mult),
                 reads=["fL5", "frec"], writes=["fc"])
            p.op("scalar", lambda e: e.activation(cbf[:], L(6), AF.Sqrt),
                 reads=["fc"], writes=[cbf])
        p.op("tensor", lambda e: e.transpose(ps_ct[:], cbf[:], cst.ident_bf[:]),
             reads=[cbf, cst.ident_bf], writes=[ps_ct])
        p.op("vector", lambda e, tg=tg, q=q: e.tensor_copy(
            cT[0:8, tg * TG + q * 128:tg * TG + (q + 1) * 128], ps_ct[0:8, :]),
            reads=[ps_ct], writes=[cT])

    for th in range(2):
        base_src = io.xacc
        p.dma("sync", lambda e, th=th: e.dma_start(
            out=macc[:].rearrange("p (c t) -> p c t", c=DC),
            in_=base_src.rearrange("(c p) t -> p c t", p=128)[:, :, th * 2048:(th + 1) * 2048]),
            reads=[base_src], writes=[macc])
        for dc in range(DC):
            ar = att_roll[0]
            p.dma("sync", lambda e, dc=dc, th=th, ar=ar: e.dma_start(
                out=ar,
                in_=io.attnout.rearrange("(c p) t -> p c t", p=128)[:, dc, th * 2048:(th + 1) * 2048]),
                reads=[io.attnout], writes=[h_roll[0]])
            p.op("vector", lambda e, dc=dc, ar=ar: e.tensor_tensor(
                macc[:, dc * 2048:(dc + 1) * 2048], macc[:, dc * 2048:(dc + 1) * 2048],
                ar, ALU.add),
                reads=[h_roll[0], macc], writes=[macc])
        with ExitStack() as gctx:
            ps_g = gctx.enter_context(nc.psum_tensor("f_ps_g", [128, 8], F32))
            ps_ct = gctx.enter_context(nc.psum_tensor("f_ps_ct", [8, 128], BF16))
            for tg4 in range(4):
                tg = th * 4 + tg4
                sl = tg4 % 2
                p.dma("scalar", lambda e, tg=tg, sl=sl: e.dma_start(
                    out=h_roll[sl][:], in_=col_ap(rhs_hbm, tg)),
                    reads=[rhs_hbm], writes=[h_roll[sl]])
                for q in range(4):
                    gate_tile(tg, q, sl, ps_g, ps_ct)
            p.barrier()
        ectx = ExitStack()
        ps_cbc = ectx.enter_context(nc.psum_tensor("f_ps_cbc", [128, TG], F32))
        ps_m1 = [ectx.enter_context(nc.psum_tensor(f"f_ps_m1_{i}", [128, TG], F32))
                 for i in range(4)]
        ps_y = [ectx.enter_context(nc.psum_tensor(f"f_ps_y{i}", [128, TG], F32))
                for i in range(2)]
        for e_i in range(8):
            w1 = w1_sb[e_i % 2]
            wsrc = "w1t0" if block0 else "w1_1"
            w2src = "w2t0" if block0 else "w2_1"
            p.dma("gpsimd", lambda e, w1=w1, e_i=e_i, wsrc=wsrc: e.dma_start(
                out=w1[:].rearrange("p (c f) -> p c f", c=DC),
                in_=io.hbm[wsrc][e_i].rearrange("(c p) f -> p c f", p=128)),
                reads=[io.hbm[wsrc]], writes=[w1])
            nw2c = 8 if block0 else 16
            p.dma("gpsimd", lambda e, e_i=e_i, w2src=w2src, nw2c=nw2c: e.dma_start(
                out=w2_sb[:].rearrange("p (c f) -> p c f", c=nw2c),
                in_=io.hbm[w2src][e_i].rearrange("(c p) f -> p c f", p=128)),
                reads=[io.hbm[w2src]], writes=[w2_sb])
            if block0:
                p.dma("gpsimd", lambda e, e_i=e_i: e.dma_start(
                    out=b1t_sb[:], in_=io.hbm["b1t0"][e_i]),
                    reads=[io.hbm["b1t0"]], writes=[b1t_sb])
            for tg4 in range(4):
                tg = th * 4 + tg4
                tsl = tg * TG
                sl = tg4 % 2
                hb = h_roll[sl]
                p.dma("scalar", lambda e, tg=tg, hb=hb: e.dma_start(
                    out=hb[:], in_=col_ap(rhs_hbm, tg)),
                    reads=[rhs_hbm], writes=[hb])
                p.op("tensor", lambda e, e_i=e_i, tsl=tsl: e.matmul(
                    ps_cbc[:], cst.sel8[0:8, e_i * 128:(e_i + 1) * 128],
                    cT[0:8, tsl:tsl + TG], start=True, stop=True),
                    reads=[cst.sel8, cT], writes=[ps_cbc])
                if not block0:
                    p.op("vector", lambda e: e.tensor_copy(scbc[:], ps_cbc[:]),
                         reads=[ps_cbc], writes=[scbc])
                for ic in range(nact):
                    if block0:
                        ps_a = ps_m1[(ic % 2) * 2]
                        ps_b = ps_m1[(ic % 2) * 2 + 1]
                        for dc in range(DC):
                            p.op("tensor", lambda e, dc=dc, ic=ic, w1=w1, hb=hb, ps_a=ps_a: e.matmul(
                                ps_a[:], w1[:, dc * 2048 + ic * 128:dc * 2048 + (ic + 1) * 128],
                                hb[:, dc * TG:(dc + 1) * TG], start=(dc == 0), stop=(dc == DC - 1)),
                                reads=[w1, hb], writes=[ps_a])
                        for dc in range(DC):
                            p.op("tensor", lambda e, dc=dc, ic=ic, w1=w1, hb=hb, ps_b=ps_b: e.matmul(
                                ps_b[:], w1[:, dc * 2048 + 1024 + ic * 128:dc * 2048 + 1024 + (ic + 1) * 128],
                                hb[:, dc * TG:(dc + 1) * TG], start=(dc == 0), stop=(dc == DC - 1)),
                                reads=[w1, hb], writes=[ps_b])
                        p.op("vector", lambda e, ic=ic, ps_a=ps_a: e.tensor_scalar(
                            gsb[:], ps_a[:], b1t_sb[:, ic:ic + 1], 7.0, ALU.add, ALU.min),
                            reads=[ps_a, b1t_sb], writes=[gsb])
                        p.op("scalar", lambda e: e.activation(sig[:], gsb[:], AF.Sigmoid,
                                                              scale=ALPHA),
                             reads=[gsb], writes=[sig])
                        p.op("vector", lambda e: e.tensor_tensor(gcb[:], gsb[:], ps_cbc[:],
                                                                 ALU.mult),
                             reads=[gsb, ps_cbc], writes=[gcb])
                        p.op("vector", lambda e, ic=ic, ps_b=ps_b: e.tensor_scalar(
                            lsb[:], ps_b[:], b1t_sb[:, 8 + ic:9 + ic], 7.0, ALU.add, ALU.min),
                            reads=[ps_b, b1t_sb], writes=[lsb])
                        p.op("vector", lambda e: e.tensor_scalar(
                            lbf[:], lsb[:], -7.0, 1.0, ALU.max, ALU.add),
                            reads=[lsb], writes=[lbf])
                        p.op("vector", lambda e: e.tensor_tensor(gs2[:], gcb[:], sig[:], ALU.mult),
                             reads=[gcb, sig], writes=[gs2])
                        p.op("vector", lambda e, ic=ic: e.tensor_tensor(
                            act[:, ic * TG:(ic + 1) * TG], gs2[:], lbf[:], ALU.mult),
                            reads=[gs2, lbf], writes=[act])
                    else:
                        pst = ps_m1[ic % 4]
                        for dc in range(DC):
                            p.op("tensor", lambda e, dc=dc, ic=ic, w1=w1, hb=hb, pst=pst: e.matmul(
                                pst[:], w1[:, dc * 2048 + ic * 128:dc * 2048 + (ic + 1) * 128],
                                hb[:, dc * TG:(dc + 1) * TG], start=(dc == 0), stop=(dc == DC - 1)),
                                reads=[w1, hb], writes=[pst])
                        p.op("vector", lambda e, pst=pst: e.tensor_tensor(
                            y1[:], pst[:], scbc[:], ALU.mult),
                            reads=[pst, scbc], writes=[y1])
                        p.op("vector", lambda e, ic=ic: e.scalar_tensor_tensor(
                            act[:, ic * TG:(ic + 1) * TG], y1[:], 0.0, y1[:],
                            ALU.max, ALU.mult),
                            reads=[y1], writes=[act])
                for dt in range(8):
                    py = ps_y[dt % 2]
                    first = True
                    if block0 and e_i == 0:
                        p.op("tensor", lambda e, dt=dt, tsl=tsl, py=py: e.matmul(
                            py[:], b2_sb[0:8, dt * 128:(dt + 1) * 128], cT[0:8, tsl:tsl + TG],
                            start=True, stop=False),
                            reads=[b2_sb, cT], writes=[py])
                        first = False
                    for ic in range(nact):
                        p.op("tensor", lambda e, dt=dt, ic=ic, py=py, first=(first and ic == 0),
                             last=(ic == nact - 1): e.matmul(
                            py[:], w2_sb[:, ic * 1024 + dt * 128:ic * 1024 + (dt + 1) * 128],
                            act[:, ic * TG:(ic + 1) * TG], start=first, stop=last),
                            reads=[w2_sb, act], writes=[py])
                    p.op("vector", lambda e, dt=dt, tg4=tg4, py=py: e.tensor_tensor(
                        macc[:, dt * 2048 + tg4 * TG:dt * 2048 + (tg4 + 1) * TG],
                        macc[:, dt * 2048 + tg4 * TG:dt * 2048 + (tg4 + 1) * TG],
                        py[:], ALU.add),
                        reads=[py, macc], writes=[macc])
        if getattr(io, "dbg_ct", None) is not None and th == 1:
            p.dma("gpsimd", lambda e: e.dma_start(out=io.dbg_ct, in_=cT[0:8, :]),
                  reads=[cT], writes=[io.dbg_ct])
        if getattr(io, "dbg_lg", None) is not None and th == 1:
            p.dma("gpsimd", lambda e: e.dma_start(out=io.dbg_lg, in_=lgt[:]),
                  reads=["fL0", "fL1", "fL2", "fL3", "fL4", "fm1", "fm2", "fw1", "fw2"],
                  writes=[io.dbg_lg])
        p.dma("sync", lambda e, th=th: e.dma_start(
            out=io.x_out.rearrange("(c p) t -> p c t", p=128)[:, :, th * 2048:(th + 1) * 2048],
            in_=macc[:].rearrange("p (c t) -> p c t", c=DC)),
            reads=[macc], writes=[io.x_out])
        p.barrier()
        ectx.close()
    p.barrier()



def emit_attn1(p, ctx, io, cst):
    nc = p.nc
    E = ctx.enter_context
    ctx_all = [E(nc.sbuf_tensor(f"a1_ctx{i}", [128, S], BF16)) for i in range(8)]
    with ExitStack() as hctx:
        EH = hctx.enter_context
        wqk_sb = EH(nc.sbuf_tensor("a1_wqk", [128, DC * 16 * 128], BF16))
        wv_sb = EH(nc.sbuf_tensor("a1_wv", [128, DC * 16 * 64], BF16))
        xsl = [EH(nc.sbuf_tensor(f"a1_xsl{i}", [128, DC * 256], BF16)) for i in range(2)]
        qf = [EH(nc.sbuf_tensor(f"a1_qf{i}", [65, S], BF16)) for i in range(2)]
        kf = [EH(nc.sbuf_tensor(f"a1_kf{i}", [65, S], BF16)) for i in range(2)]
        vfm = EH(nc.sbuf_tensor("a1_vfm", [64, S], BF16))
        stage = EH(nc.sbuf_tensor("a1_stage", [128, S], BF16))
        va = [EH(nc.sbuf_tensor(f"a1_va{i}", [128, 32 * 65], BF16)) for i in range(2)]
        qsq = EH(nc.sbuf_tensor("a1_qsq", [64, TG], BF16))
        u65 = EH(nc.sbuf_tensor("a1_u65", [65, TG], F32))
        krow = EH(nc.sbuf_tensor("a1_krow", [65, 20], F32))
        e_sb = [EH(nc.sbuf_tensor(f"a1_e{i}", [128, 4 * TG], BF16)) for i in range(2)]
        rz = EH(nc.sbuf_tensor("a1_rz", [65, TG], BF16))
        rzf = EH(nc.sbuf_tensor("a1_rzf", [65, TG], F32))
        cx_sb = EH(nc.sbuf_tensor("a1_cx", [128, TG], BF16))
        cxb = EH(nc.sbuf_tensor("a1_cxb", [64, TG], BF16))
        ps_p = EH(nc.psum_tensor("a1_ps_p", [128, 256], F32))
        ps_tp = [EH(nc.psum_tensor("a1_ps_tp0", [128, 256], BF16))] * 2
        ps_misc = EH(nc.psum_tensor("a1_ps_misc", [128, TG], F32))
        ps_st = ps_misc
        ps_s4 = EH(nc.psum_tensor("a1_ps_s4", [128, 4 * TG], F32))
        ps_ctx = EH(nc.psum_tensor("a1_ps_ctx", [65, TG], F32))
        ps_rzb = ps_misc

        p.dma("gpsimd", lambda e: e.dma_start(
            out=wqk_sb[:].rearrange("p (c f) -> p c f", c=DC),
            in_=io.hbm["wqk1"].rearrange("(c p) f -> p c f", p=128)),
            reads=[io.hbm["wqk1"]], writes=[wqk_sb])
        p.dma("gpsimd", lambda e: e.dma_start(
            out=wv_sb[:].rearrange("p (c f) -> p c f", c=DC),
            in_=io.hbm["wv1"].rearrange("(c p) f -> p c f", p=128)),
            reads=[io.hbm["wv1"]], writes=[wv_sb])
        for i in range(2):
            p.op("vector", lambda e, i=i: e.memset(
                va[i][:].rearrange("p (t c) -> p t c", c=65)[:, :, 64:65], 1.0),
                writes=[va[i]])
            p.op("vector", lambda e, i=i: e.memset(kf[i][64:65, :], 1.0), writes=[kf[i]])

        for h in range(16):
            h2 = h % 2
            xs = xsl[h2]
            hp = h // 2
            p.dma("sync", lambda e, xs=xs, h=h: e.dma_start(
                out=xs[:].rearrange("p (c t) -> p c t", c=DC),
                in_=io.xn.rearrange("(c p) t -> p c t", p=128)[:, :, h * 256:(h + 1) * 256]),
                reads=[io.xn], writes=[xs])
            for r in range(16):
                for dc in range(DC):
                    p.op("tensor", lambda e, dc=dc, r=r, xs=xs: e.matmul(
                        ps_p[:], wqk_sb[:, dc * 2048 + r * 128:dc * 2048 + (r + 1) * 128],
                        xs[:, dc * 256:(dc + 1) * 256], start=(dc == 0), stop=(dc == DC - 1)),
                        reads=[wqk_sb, xs], writes=[ps_p])
                p.op("vector", lambda e, r=r, h2=h2: e.tensor_scalar(
                    qf[h2][0:64, r * 256:(r + 1) * 256], ps_p[0:64, :],
                    cst.bqk1[0:64, r:r + 1], None, ALU.add),
                    reads=[ps_p, cst.bqk1], writes=[qf[h2]])
                p.op("vector", lambda e, r=r: e.tensor_scalar(
                    stage[64:128, r * 256:(r + 1) * 256], ps_p[64:128, :],
                    cst.bqk1[64:128, r:r + 1], None, ALU.add),
                    reads=[ps_p, cst.bqk1], writes=[stage])
            p.dma("sync", lambda e, h2=h2: e.dma_start(
                out=kf[h2][0:64, :], in_=stage[64:128, :]),
                reads=[stage], writes=[kf[h2]])
            for rp in range(8):
                for dc in range(DC):
                    p.op("tensor", lambda e, dc=dc, rp=rp, xs=xs: e.matmul(
                        ps_p[:], wv_sb[:, dc * 1024 + rp * 128:dc * 1024 + (rp + 1) * 128],
                        xs[:, dc * 256:(dc + 1) * 256], start=(dc == 0), stop=(dc == DC - 1)),
                        reads=[wv_sb, xs], writes=[ps_p])
                p.op("vector", lambda e, rp=rp: e.tensor_scalar(
                    vfm[0:64, (2 * rp) * 256:(2 * rp + 1) * 256], ps_p[0:64, :],
                    cst.bv1[0:64, rp:rp + 1], None, ALU.add),
                    reads=[ps_p, cst.bv1], writes=[vfm])
                p.op("vector", lambda e, rp=rp: e.tensor_scalar(
                    stage[64:128, rp * 256:(rp + 1) * 256], ps_p[64:128, :],
                    cst.bv1[64:128, rp:rp + 1], None, ALU.add),
                    reads=[ps_p, cst.bv1], writes=[stage])
            p.dma("sync", lambda e: e.dma_start(
                out=vfm[0:64, :].rearrange("p (r two j) -> p r two j", two=2, j=256)[:, :, 1, :],
                in_=stage[64:128, 0:2048]),
                reads=[stage], writes=[vfm])
            for g in range(8):
                tp = ps_tp[g % 2]
                for t4 in range(4):
                    tt = g * 4 + t4
                    p.op("tensor", lambda e, tp=tp, t4=t4, tt=tt: e.transpose(
                        tp[:, t4 * 64:(t4 + 1) * 64],
                        vfm[0:64, tt * 128:(tt + 1) * 128], cst.ident_bf[0:64, 0:64]),
                        reads=[vfm, cst.ident_bf], writes=[tp])
                p.op("vector", lambda e, tp=tp, g=g, h2=h2: e.tensor_copy(
                    va[h2][:, g * 4 * 65:(g + 1) * 4 * 65].rearrange(
                        "p (t c) -> p t c", c=65)[:, :, 0:64],
                    tp[:].rearrange("p (t c) -> p t c", c=64)),
                    reads=[tp], writes=[va[h2]])
            for c in range(8):
                p.op("scalar", lambda e, c=c, h2=h2: e.activation(
                    qsq[:], kf[h2][0:64, c * TG:(c + 1) * TG], AF.Square),
                    reads=[kf[h2]], writes=[qsq])
                p.op("tensor", lambda e: e.matmul(
                    ps_st[64:65, :], cst.ones_bf[0:64, 0:1], qsq[:],
                    start=True, stop=True, tile_position=(0, 64)),
                    reads=[qsq, cst.ones_bf], writes=[ps_st])
                p.op("vector", lambda e, c=c: e.reduce_max(
                    krow[64:65, c:c + 1], ps_st[64:65, :], AX.X),
                    reads=[ps_st], writes=[krow])
            p.op("vector", lambda e: e.reduce_max(
                krow[64:65, 8:9], krow[64:65, 0:8], AX.X), reads=[krow], writes=["a1km"])
            p.op("scalar", lambda e: e.activation(
                krow[64:65, 9:10], krow[64:65, 8:9], AF.Sqrt),
                reads=["a1km"], writes=["a1km2"])
            for c in range(8):
                p.op("scalar", lambda e, c=c, h2=h2: e.activation(
                    qsq[:], qf[h2][0:64, c * TG:(c + 1) * TG], AF.Square),
                    reads=[qf[h2]], writes=[qsq])
                p.op("tensor", lambda e: e.matmul(
                    ps_st[64:65, :], cst.ones_bf[0:64, 0:1], qsq[:],
                    start=True, stop=True, tile_position=(0, 64)),
                    reads=[qsq, cst.ones_bf], writes=[ps_st])
                p.op("vector", lambda e, c=c: e.reduce_max(
                    krow[64:65, 10 + c:11 + c], ps_st[64:65, :], AX.X),
                    reads=[ps_st], writes=["a1qm"])
            p.op("vector", lambda e: e.reduce_max(
                krow[64:65, 2:3], krow[64:65, 10:14], AX.X), reads=["a1qm"], writes=["a1qx"])
            p.op("vector", lambda e: e.reduce_max(
                krow[64:65, 3:4], krow[64:65, 14:18], AX.X), reads=["a1qm"], writes=["a1qy"])
            p.op("vector", lambda e: e.tensor_tensor(
                krow[64:65, 4:5], krow[64:65, 2:3], krow[64:65, 3:4], ALU.max),
                reads=["a1qx", "a1qy"], writes=["a1qz"])
            p.op("scalar", lambda e: e.activation(
                krow[64:65, 5:6], krow[64:65, 4:5], AF.Sqrt),
                reads=["a1qz"], writes=["a1qs"])
            p.op("vector", lambda e: e.scalar_tensor_tensor(
                krow[64:65, 6:7], krow[64:65, 5:6], -1.0, krow[64:65, 9:10],
                ALU.mult, ALU.mult),
                reads=["a1qs", "a1km2"], writes=["a1bnd"])
            for qb in range(8):
                p.op("vector", lambda e, h2=h2, qb=qb: e.tensor_copy(
                    qf[h2][64:65, qb * TG:(qb + 1) * TG],
                    krow[64:65, 6:7].to_broadcast([1, TG])),
                    reads=["a1bnd"], writes=[qf[h2]])
                for k in range(9):
                    if k < 8:
                        eb = e_sb[k % 2]
                        for half in range(4):
                            kt = 4 * k + half
                            p.op("tensor", lambda e, h2=h2, kt=kt, qb=qb, half=half: e.matmul(
                                ps_s4[:, half * TG:(half + 1) * TG],
                                kf[h2][0:65, kt * 128:(kt + 1) * 128],
                                qf[h2][0:65, qb * TG:(qb + 1) * TG], start=True, stop=True),
                                reads=[kf[h2], qf[h2]], writes=[ps_s4])
                        p.op("scalar", lambda e, eb=eb: e.activation(
                            eb[:], ps_s4[:], AF.Exp, scale=0.125),
                            reads=[ps_s4], writes=[eb])
                    if k > 0:
                        eb = e_sb[(k - 1) % 2]
                        for half in range(4):
                            kt = 4 * (k - 1) + half
                            p.op("tensor", lambda e, eb=eb, h2=h2, kt=kt, half=half,
                                 last=(kt == 31): e.matmul(
                                ps_ctx[:], va[h2][:, kt * 65:(kt + 1) * 65],
                                eb[:, half * TG:(half + 1) * TG],
                                start=(kt == 0), stop=last),
                                reads=[eb, va[h2]], writes=[ps_ctx])
                p.op("vector", lambda e: e.reciprocal(rzf[64:65, :], ps_ctx[64:65, :]),
                     reads=[ps_ctx], writes=[rzf])
                p.op("vector", lambda e: e.tensor_copy(rz[64:65, :], rzf[64:65, :]),
                     reads=[rzf], writes=[rz])
                p.op("tensor", lambda e: e.matmul(
                    ps_rzb[0:64, :], cst.ones_bf[64:65, 0:64], rz[64:65, :],
                    start=True, stop=True),
                    reads=[rz, cst.ones_bf], writes=[ps_rzb])
                p.op("vector", lambda e: e.tensor_copy(cx_sb[0:64, :], ps_ctx[0:64, :]),
                     reads=[ps_ctx], writes=[cx_sb])
                if h2 == 0:
                    p.op("vector", lambda e, hp=hp, qb=qb: e.tensor_tensor(
                        ctx_all[hp][0:64, qb * TG:(qb + 1) * TG],
                        cx_sb[0:64, :], ps_rzb[0:64, :], ALU.mult),
                        reads=[cx_sb, ps_rzb], writes=[ctx_all[hp]])
                else:
                    p.op("vector", lambda e: e.tensor_tensor(
                        cxb[0:64, :], cx_sb[0:64, :], ps_rzb[0:64, :], ALU.mult),
                        reads=[cx_sb, ps_rzb], writes=[cxb])
                    p.dma("sync", lambda e, hp=hp, qb=qb: e.dma_start(
                        out=ctx_all[hp][64:128, qb * TG:(qb + 1) * TG], in_=cxb[0:64, :]),
                        reads=[cxb], writes=[ctx_all[hp]])
        p.barrier()
    with ExitStack() as octx:
        EO = octx.enter_context
        wo_sb = EO(nc.sbuf_tensor("a1_wo", [128, 8 * 1024], BF16))
        ot = [EO(nc.sbuf_tensor(f"a1_ot{i}", [128, TG], F32)) for i in range(2)]
        ps_o = [EO(nc.psum_tensor(f"a1_ps_o{i}", [128, TG], F32)) for i in range(2)]
        p.dma("gpsimd", lambda e: e.dma_start(
            out=wo_sb[:].rearrange("p (c f) -> p c f", c=8),
            in_=io.hbm["w_out1"].rearrange("(c p) f -> p c f", p=128)),
            reads=[io.hbm["w_out1"]], writes=[wo_sb])
        for mt in range(8):
            for tg in range(8):
                sl = (mt * 8 + tg) % 2
                for ec in range(8):
                    rhs = ctx_all[ec][:].rearrange("p (r j) -> p j r", r=16)[
                        :, tg * 32:(tg + 1) * 32, :]
                    p.op("tensor", lambda e, sl=sl, ec=ec, mt=mt, rhs=rhs: e.matmul(
                        ps_o[sl][:], wo_sb[:, ec * 1024 + mt * 128:ec * 1024 + (mt + 1) * 128],
                        rhs, start=(ec == 0), stop=(ec == 7)),
                        reads=[wo_sb, ctx_all[ec]], writes=[ps_o[sl]])
                p.op("vector", lambda e, sl=sl, mt=mt: e.tensor_scalar(
                    ot[sl][:], ps_o[sl][:], cst.b_out1[:, mt:mt + 1], None, ALU.add),
                    reads=[ps_o[sl], cst.b_out1], writes=[ot[sl]])
                p.dma("sync", lambda e, sl=sl, mt=mt, tg=tg: e.dma_start(
                    out=io.attnout.rearrange("(c p) t -> p c t", p=128)[:, mt, tg * TG:(tg + 1) * TG],
                    in_=ot[sl][:]),
                    reads=[ot[sl]], writes=[io.attnout])
    p.barrier()



_PHASES = {
    "ln0": dict(
        shared=["ones_col", "ones_row", "nw0"],
        per_core=["x"],
        outs={"xacc": ((D, S), np.float32), "h": ((D, S), BF), "xn": ((D, S), BF)},
        consts=["ones_col", "ones_row", "nw0"]),
    "ln1": dict(
        shared=["ones_col", "ones_row"],
        per_core=["x"],
        outs={"xn": ((D, S), BF)},
        consts=["ones_col", "ones_row"]),
    "attn0": dict(
        shared=["w_qkv0", "b_qkv0", "bias_tbl0", "iota_loc", "w_out0", "b_out0",
                "ones_bf", "ident_bf", "cos_tbl", "sin_tbl", "masksneg"],
        per_core=["xn"],
        outs={"attnout": ((D, S), np.float32)},
        consts=["ones_bf", "ident_bf", "cos_tbl", "sin_tbl", "masksneg", "b_out0"]),
    "attn1": dict(
        shared=["wqk1", "wv1", "bqk1", "bv1", "w_out1", "b_out1", "ones_bf", "ident_bf"],
        per_core=["xn"],
        outs={"attnout": ((D, S), np.float32)},
        consts=["ones_bf", "ident_bf", "bqk1", "bv1", "b_out1"]),
    "ffn0": dict(
        shared=["gate_w0", "gate_b0", "w1t0", "b1t0", "w2t0", "b2_0", "sel8",
                "ones_bf", "ident_bf"],
        per_core=["h", "xacc", "attnout"],
        outs={"x_out": ((D, S), np.float32), "dbg_ct": ((8, S), BF),
              "dbg_lg": ((128, 128), np.float32)},
        consts=["sel8", "ones_bf", "ident_bf"]),
    "ffn1": dict(
        shared=["wg1", "w1_1", "w2_1", "sel8", "ones_bf", "ident_bf"],
        per_core=["h", "xacc", "attnout"],
        outs={"x_out": ((D, S), np.float32)},
        consts=["sel8", "ones_bf", "ident_bf"]),
}


def build_phase(phase, hostd, per_core_shapes):
    cfg = _PHASES[phase]
    nc = bass.Bass("TRN2", target_bir_lowering=False)
    aps = {}
    for name in cfg["shared"]:
        arr = hostd[name]
        dt = BF16 if arr.dtype == BF else F32
        aps[name] = nc.dram_tensor(name, list(arr.shape), dt, kind="ExternalInput").ap()
    for name in cfg["per_core"]:
        shape, dtype = per_core_shapes[name]
        dt = BF16 if dtype == BF else F32
        aps[name] = nc.dram_tensor(name, list(shape), dt, kind="ExternalInput").ap()
    for name, (shape, dtype) in cfg["outs"].items():
        dt = BF16 if dtype == BF else F32
        aps[name] = nc.dram_tensor(name, list(shape), dt, kind="ExternalOutput").ap()
    p = Prog(nc)
    ctx = ExitStack()
    with ctx:
        cst = emit_consts(p, ctx, aps, cfg["consts"])
        io = NS(hbm=aps,
                x_in=aps.get("x"), xn=aps.get("xn"), xacc=aps.get("xacc"),
                h=aps.get("h"), attnout=aps.get("attnout"), x_out=aps.get("x_out"),
                dbg_ct=aps.get("dbg_ct"), dbg_lg=aps.get("dbg_lg"))
        if phase == "ln0":
            emit_ln(p, ctx, io, cst, block0=True)
        elif phase == "ln1":
            emit_ln(p, ctx, io, cst, block0=False)
        elif phase == "attn0":
            emit_attn0(p, ctx, io, cst)
        elif phase == "attn1":
            emit_attn1(p, ctx, io, cst)
        elif phase == "ffn0":
            emit_ffn(p, ctx, io, cst, block0=True)
        elif phase == "ffn1":
            emit_ffn(p, ctx, io, cst, block0=False)
        p.barrier()
        p.finalize(ctx)
    return nc


def run_phase(phase, hostd, per_core, n_cores=8, **kw):
    cfg = _PHASES[phase]
    shapes = {k: (v[0].shape, v[0].dtype.type if v[0].dtype != np.dtype(BF) else BF)
              for k, v in per_core.items()}
    shapes = {k: (v[0].shape, BF if v[0].dtype == np.dtype(BF) else np.float32)
              for k, v in per_core.items()}
    nc = build_phase(phase, hostd, shapes)
    shared = {k: hostd[k] for k in cfg["shared"]}
    in_maps = []
    for b in range(n_cores):
        m = dict(shared)
        for k, v in per_core.items():
            m[k] = np.ascontiguousarray(v[b])
        in_maps.append(m)
    res = run_bass_kernel_spmd(nc, in_maps, core_ids=list(range(n_cores)), **kw)
    return res



_ALL_SHARED = ["w_qkv0", "b_qkv0", "bias_tbl0", "iota_loc", "w_out0", "b_out0",
               "ones_col", "ones_row", "nw0", "ones_bf", "ident_bf", "cos_tbl",
               "sin_tbl", "masksneg", "gate_w0", "gate_b0", "w1t0", "b1t0",
               "w2t0", "b2_0", "sel8", "wqk1", "wv1", "bqk1", "bv1", "w_out1",
               "b_out1", "wg1", "w1_1", "w2_1"]
_ALL_CONSTS = ["ones_col", "ones_row", "nw0", "ones_bf", "ident_bf", "cos_tbl",
               "sin_tbl", "masksneg", "b_out0", "sel8", "bqk1", "bv1", "b_out1"]


def _uniquify_names(nc):
    ctr = [0]
    orig_s, orig_p = nc.sbuf_tensor, nc.psum_tensor

    def s(name, *a, **k):
        ctr[0] += 1
        return orig_s(f"{name}_u{ctr[0]}", *a, **k)

    def q(name, *a, **k):
        ctr[0] += 1
        return orig_p(f"{name}_u{ctr[0]}", *a, **k)

    nc.sbuf_tensor = s
    nc.psum_tensor = q


def build_full(hostd):
    nc = bass.Bass("TRN2", target_bir_lowering=False)
    _uniquify_names(nc)
    aps = {}
    for name in _ALL_SHARED:
        arr = hostd[name]
        dt = BF16 if arr.dtype == BF else F32
        aps[name] = nc.dram_tensor(name, list(arr.shape), dt, kind="ExternalInput").ap()
    aps["x"] = nc.dram_tensor("x", [D, S], F32, kind="ExternalInput").ap()
    aps["y"] = nc.dram_tensor("y", [D, S], F32, kind="ExternalOutput").ap()
    scr = {}
    scr["xacc"] = nc.dram_tensor("s_xacc", [D, S], F32, kind="Internal").ap()
    scr["h0"] = nc.dram_tensor("s_h0", [D, S], BF16, kind="Internal").ap()
    scr["xn0"] = nc.dram_tensor("s_xn0", [D, S], BF16, kind="Internal").ap()
    scr["att0"] = nc.dram_tensor("s_att0", [D, S], F32, kind="Internal").ap()
    scr["x1"] = nc.dram_tensor("s_x1", [D, S], F32, kind="Internal").ap()
    scr["xn1"] = nc.dram_tensor("s_xn1", [D, S], BF16, kind="Internal").ap()
    scr["att1"] = nc.dram_tensor("s_att1", [D, S], F32, kind="Internal").ap()

    p = Prog(nc)
    with ExitStack() as ctx:
        base = dict(hbm=aps, dbg_ct=None, dbg_lg=None)
        with ExitStack() as c0:
            cst = emit_consts(p, c0, aps, _PHASES["ln0"]["consts"])
            emit_ln(p, c0, NS(**base, x_in=aps["x"], xn=scr["xn0"], xacc=scr["xacc"],
                              h=scr["h0"]), cst, block0=True)
            p.barrier()
        with ExitStack() as c1:
            cst = emit_consts(p, c1, aps, _PHASES["attn0"]["consts"])
            emit_attn0(p, c1, NS(**base, xn=scr["xn0"], attnout=scr["att0"]), cst)
            p.barrier()
        with ExitStack() as c2:
            cst = emit_consts(p, c2, aps, _PHASES["ffn0"]["consts"])
            emit_ffn(p, c2, NS(**base, h=scr["h0"], xacc=scr["xacc"],
                               attnout=scr["att0"], x_out=scr["x1"]), cst, block0=True)
            p.barrier()
        with ExitStack() as c3:
            cst = emit_consts(p, c3, aps, _PHASES["ln1"]["consts"])
            emit_ln(p, c3, NS(**base, x_in=scr["x1"], xn=scr["xn1"]), cst, block0=False)
            p.barrier()
        with ExitStack() as c4:
            cst = emit_consts(p, c4, aps, _PHASES["attn1"]["consts"])
            emit_attn1(p, c4, NS(**base, xn=scr["xn1"], attnout=scr["att1"]), cst)
            p.barrier()
        with ExitStack() as c5:
            cst = emit_consts(p, c5, aps, _PHASES["ffn1"]["consts"])
            emit_ffn(p, c5, NS(**base, h=scr["xn1"], xacc=scr["x1"],
                               attnout=scr["att1"], x_out=aps["y"]), cst, block0=False)
            p.barrier()
        p.finalize(ctx)
    return nc


def kernel_fused(inputs):
    hostd = host_prep(inputs)
    x = inputs["L0_attn_x"].astype(np.float32, copy=False)
    n = 8
    nc = build_full(hostd)
    shared = {k: hostd[k] for k in _ALL_SHARED}
    in_maps = []
    for b in range(n):
        m = dict(shared)
        m["x"] = np.ascontiguousarray(x[b].T)
        in_maps.append(m)
    res = run_bass_kernel_spmd(nc, in_maps, core_ids=list(range(n)))
    return np.stack([np.ascontiguousarray(r["y"].T) for r in res.results])


def kernel(**inputs):
    inputs = {k: np.asarray(v) for k, v in inputs.items()}
    return kernel_fused(inputs)


def kernel_multilaunch(**inputs):
    inputs = {k: np.asarray(v) for k, v in inputs.items()}
    hostd = host_prep(inputs)
    x = inputs["L0_attn_x"].astype(np.float32, copy=False)
    n = 8
    xT = [np.ascontiguousarray(x[b].T) for b in range(n)]
    r0 = run_phase("ln0", hostd, {"x": xT}).results
    r1 = run_phase("attn0", hostd, {"xn": [r["xn"] for r in r0]}).results
    r2 = run_phase("ffn0", hostd, {"h": [r["h"] for r in r0],
                                   "xacc": [r["xacc"] for r in r0],
                                   "attnout": [r["attnout"] for r in r1]}).results
    x1 = [r["x_out"] for r in r2]
    r3 = run_phase("ln1", hostd, {"x": x1}).results
    xn1 = [r["xn"] for r in r3]
    r4 = run_phase("attn1", hostd, {"xn": xn1}).results
    r5 = run_phase("ffn1", hostd, {"h": xn1, "xacc": x1,
                                   "attnout": [r["attnout"] for r in r4]}).results
    return np.stack([np.ascontiguousarray(r["x_out"].T) for r in r5])


# version 14 (final)
# speedup vs baseline: 1.0764x; 1.0764x over previous
import numpy as np
import ml_dtypes
from contextlib import ExitStack
import concourse.bass as bass
import concourse.mybir as mybir
from concourse.bass_utils import run_bass_kernel_spmd

F32 = mybir.dt.float32
BF16 = mybir.dt.bfloat16
AX = mybir.AxisListType
AF = mybir.ActivationFunctionType
ALU = mybir.AluOpType
BF = ml_dtypes.bfloat16

S = 4096
D = 1024
DC = 8
TG = 512
NTG = S // TG
NH = 16
DH = 64
LN_EPS = 1e-5
RMS_EPS = float(np.finfo(np.float32).eps)
SQ192 = float(np.sqrt(np.float64(np.float32(192.0))))
ISQ192 = float(1.0 / np.sqrt(np.float32(192.0)))
ALPHA = 1.702
MAX_BIAS = 8.0

ENGINES = ("tensor", "vector", "scalar", "gpsimd", "sync")


def _slopes16():
    start = 2.0 ** (-(2.0 ** -(np.log2(16) - 3)))
    return np.array([start * start**i for i in range(16)], dtype=np.float64)


class Prog:

    def __init__(self, nc):
        self.nc = nc
        self.recs = {e: [] for e in ENGINES}
        self.w = {}
        self.r = {}

    @staticmethod
    def _key(x):
        return x if isinstance(x, str) else id(x)

    def op(self, eng, fn, reads=(), writes=(), extra=(), dma=False):
        waits = set(t for t in extra if t is not None)
        for x in reads:
            k = self._key(x)
            t = self.w.get(k)
            if t is not None:
                waits.add(t)
        for x in writes:
            k = self._key(x)
            t = self.w.get(k)
            if t is not None:
                waits.add(t)
            for t2 in self.r.get(k, ()):
                waits.add(t2)
        tok = (eng, len(self.recs[eng]), dma)
        self.recs[eng].append([fn, tuple(waits), dma, False])
        for t in waits:
            if t[0] != eng or t[2] or eng != "tensor":
                self.recs[t[0]][t[1]][3] = True
        for x in reads:
            self.r.setdefault(self._key(x), []).append(tok)
        for x in writes:
            k = self._key(x)
            self.w[k] = tok
            self.r[k] = []
        return tok

    def dma(self, eng, fn, reads=(), writes=(), extra=()):
        return self.op(eng, fn, reads, writes, extra, dma=True)

    def barrier(self):
        lasts = []
        for e in ENGINES:
            if self.recs[e]:
                i = len(self.recs[e]) - 1
                lasts.append((e, i, self.recs[e][i][2]))
        for e in ENGINES:
            w = tuple(t for t in lasts if t[0] != e or t[2])
            tok = self.op(e, lambda eng: eng.nop(), extra=w)
        bar = [(e, len(self.recs[e]) - 1, False) for e in ENGINES]
        for k in list(self.w.keys()):
            self.w[k] = bar[0]
            self.r[k] = [bar[ENGINES.index(e)] for e in ENGINES[1:]]
        return bar

    def finalize(self, ctx):
        nc = self.nc
        sems = {}
        for e in ENGINES:
            sems[(e, False)] = ctx.enter_context(nc.semaphore(f"sem_{e}_c"))
            sems[(e, True)] = ctx.enter_context(nc.semaphore(f"sem_{e}_d"))
        val = {}
        for e in ENGINES:
            c_cnt = 0
            d_cnt = 0
            for i, rec in enumerate(self.recs[e]):
                if rec[2]:
                    d_cnt += 16
                    val[(e, i)] = ((e, True), d_cnt)
                elif rec[3]:
                    c_cnt += 1
                    val[(e, i)] = ((e, False), c_cnt)
        block = ctx.enter_context(nc.Block())

        def make_section(e):
            recs = self.recs[e]

            def section(eng):
                hw = {}
                for i, (fn, waits, is_dma, marked) in enumerate(recs):
                    need = {}
                    for t in waits:
                        if t[0] == e and not t[2] and e == "tensor":
                            continue
                        sk, v = val[(t[0], t[1])]
                        if need.get(sk, 0) < v:
                            need[sk] = v
                    for sk in sorted(need, key=lambda s: (s[0], s[1])):
                        v = need[sk]
                        if hw.get(sk, 0) < v:
                            eng.wait_ge(sems[sk], v)
                            hw[sk] = v
                    inst = fn(eng)
                    if is_dma:
                        inst.then_inc(sems[(e, True)], 16)
                    elif marked:
                        inst.then_inc(sems[(e, False)], 1)

            return section

        for e in ENGINES:
            if self.recs[e]:
                getattr(block, e)(make_section(e))

    def stats(self):
        return {e: len(self.recs[e]) for e in ENGINES}


def col_ap(hbm, tg, width=TG, nd=DC):
    a = hbm.rearrange("(c p) t -> p c t", p=128)
    return a[:, :, tg * width:(tg + 1) * width]


class NS:
    def __init__(self, **kw):
        self.__dict__.update(kw)



def host_prep(i):
    d = {}
    f32 = np.float32
    d["w_qkv0"] = np.ascontiguousarray(i["L0_attn_w_qkv"].reshape(D, NH * 192))
    d["b_qkv0"] = np.ascontiguousarray(np.asarray(i["L0_attn_b_qkv"]).reshape(1, NH * 192))
    d["w_out0"] = np.ascontiguousarray(i["L0_attn_w_out"].reshape(D, D))
    d["b_out0"] = np.ascontiguousarray(np.asarray(i["L0_attn_b_out"]).reshape(8, 128).T)
    d["nw0"] = np.ascontiguousarray(np.asarray(i["L0_ffn_norm_w"]).reshape(8, 128).T)
    d["gate_w0"] = np.ascontiguousarray(np.asarray(i["L0_ffn_gate_w"]).T)
    d["gate_b0"] = np.ascontiguousarray(np.asarray(i["L0_ffn_gate_b"]).reshape(1, 8))
    w1t = np.transpose(i["L0_ffn_mlp1_w"], (0, 2, 1))
    d["w1t0"] = np.ascontiguousarray(np.concatenate([w1t[:, :, 0::2], w1t[:, :, 1::2]], axis=2))
    b1 = np.asarray(i["L0_ffn_mlp1_b"])
    b1r = np.concatenate([b1[:, 0::2], b1[:, 1::2]], axis=1)
    d["b1t0"] = np.ascontiguousarray(np.transpose(b1r.reshape(8, 16, 128), (0, 2, 1)))
    d["w2t0"] = np.ascontiguousarray(np.transpose(i["L0_ffn_mlp2_w"], (0, 2, 1)))
    d["b2_0"] = np.ascontiguousarray(np.asarray(i["L0_ffn_mlp2_b"]))

    Wq = np.asarray(i["L1_attn_W_qkv"])
    bq = np.asarray(i["L1_attn_b_qkv"])
    wqk = np.zeros((D, 16 * 128), dtype=f32)
    wv = np.zeros((D, 16 * 64), dtype=f32)
    bqk = np.zeros((128, 16), dtype=f32)
    bv = np.zeros((128, 8), dtype=f32)
    for r in range(16):
        wqk[:, r * 128:r * 128 + 64] = Wq[r * 192:r * 192 + 64, :].T
        wqk[:, r * 128 + 64:r * 128 + 128] = Wq[r * 192 + 64:r * 192 + 128, :].T
        wv[:, r * 64:(r + 1) * 64] = Wq[r * 192 + 128:r * 192 + 192, :].T
        bqk[0:64, r] = bq[r * 192:r * 192 + 64]
        bqk[64:128, r] = bq[r * 192 + 64:r * 192 + 128]
        bv[(r % 2) * 64:(r % 2) * 64 + 64, r // 2] = bq[r * 192 + 128:r * 192 + 192]
    d["wqk1"], d["wv1"], d["bqk1"], d["bv1"] = wqk, wv, bqk, bv
    d["w_out1"] = np.ascontiguousarray(np.asarray(i["L1_attn_W_out"]).T)
    d["b_out1"] = np.ascontiguousarray(np.asarray(i["L1_attn_b_out"]).reshape(8, 128).T)
    d["wg1"] = np.ascontiguousarray(np.asarray(i["L1_ffn_Wg"]))
    d["w1_1"] = np.ascontiguousarray(np.asarray(i["L1_ffn_W1"]))
    d["w2_1"] = np.ascontiguousarray(np.asarray(i["L1_ffn_W2"]))

    d["ones_col"] = np.ones((128, 1), dtype=f32)
    d["ones_row"] = np.ones((1, 128), dtype=f32)
    d["ident_bf"] = np.eye(128, dtype=BF)
    d["ones_bf"] = np.ones((128, 128), dtype=BF)
    invf = 10000.0 ** (-np.arange(0, 32, dtype=np.float64) / 32.0)
    fr = np.outer(np.arange(S, dtype=np.float64), invf)
    cs = np.cos(fr).astype(f32).reshape(32, 128, 32)
    sn = np.sin(fr).astype(f32).reshape(32, 128, 32)
    d["cos_tbl"] = np.ascontiguousarray(np.transpose(cs, (1, 0, 2)).reshape(128, 1024))
    d["sin_tbl"] = np.ascontiguousarray(np.transpose(sn, (1, 0, 2)).reshape(128, 1024))
    sl = _slopes16() * MAX_BIAS
    bt = np.zeros((8, 128, 512), dtype=f32)
    for h in range(16):
        for kt in range(32):
            for qb in range(8):
                bt[h // 2, :, (h % 2) * 256 + kt * 8 + qb] = (
                    sl[h] * (kt * 128 + np.arange(128) - 512.0 * qb))
    d["bias_tbl0"] = bt
    d["iota_loc"] = np.arange(TG, dtype=f32).reshape(1, TG)
    p_ = np.arange(128)[:, None]
    f_ = np.arange(TG)[None, :]
    d["masks"] = np.stack(
        [(f_ >= p_ + r * 128).astype(f32) for r in range(4)]).astype(BF)
    d["masksneg"] = np.ascontiguousarray(np.concatenate(
        [np.where(f_ >= p_ + r * 128, 0.0, -1.0e38).astype(f32) for r in range(4)],
        axis=1))
    sel = np.zeros((8, 8 * 128), dtype=BF)
    for e in range(8):
        sel[e, e * 128:(e + 1) * 128] = 1
    d["sel8"] = sel
    return d



def emit_consts(p, ctx, hbm, need):
    nc = p.nc
    E = ctx.enter_context
    shapes = {
        "ones_col": ([128, 1], F32), "ones_row": ([1, 128], F32),
        "ident_bf": ([128, 128], BF16), "ones_bf": ([128, 128], BF16),
        "cos_tbl": ([128, 1024], F32), "sin_tbl": ([128, 1024], F32),
        "masks": ([128, 4 * TG], BF16),
        "masksneg": ([128, 4 * TG], F32),
        "sel8": ([8, 8 * 128], BF16),
        "nw0": ([128, 8], F32), "b_out0": ([128, 8], F32),
        "gate_b0": ([1, 8], F32),
        "bqk1": ([128, 16], F32), "bv1": ([128, 8], F32), "b_out1": ([128, 8], F32),
    }
    cst = NS()
    for name in need:
        shp, dt = shapes[name]
        buf = E(nc.sbuf_tensor("c_" + name, shp, dt))
        setattr(cst, name, buf)
        src = hbm[name]
        if name == "masks":
            src = src.rearrange("r p t -> p (r t)")
        p.dma("sync", lambda e, b=buf, s=src: e.dma_start(out=b[:], in_=s), writes=[buf])
    return cst



def emit_ln(p, ctx, io, cst, block0):
    nc = p.nc
    E = ctx.enter_context
    x_col = [E(nc.sbuf_tensor(f"ln_x{i}", [128, DC * TG], F32)) for i in range(2)]
    xn_col = E(nc.sbuf_tensor("ln_xn", [128, DC * TG], F32))
    sqx = E(nc.sbuf_tensor("ln_sqx", [128, TG], F32))
    hsc = E(nc.sbuf_tensor("ln_hsc", [128, TG], F32))
    out_f = [E(nc.sbuf_tensor(f"ln_of{i}", [128, DC * TG], F32)) for i in range(2)]
    out_h = [E(nc.sbuf_tensor(f"ln_oh{i}", [128, DC * TG], BF16)) for i in range(2)]
    out_xn = [E(nc.sbuf_tensor(f"ln_oxn{i}", [128, DC * TG], BF16)) for i in range(2)]
    rows = E(nc.sbuf_tensor("ln_rows", [1, 4 * TG], F32))
    ps_sum = E(nc.psum_tensor("ln_ps_sum", [1, TG], F32))
    ps_sq = E(nc.psum_tensor("ln_ps_sq", [1, TG], F32))
    ps_mbc = E(nc.psum_tensor("ln_ps_mbc", [128, TG], F32))
    ps_rbc = E(nc.psum_tensor("ln_ps_rbc", [128, TG], F32))
    ps_rms = E(nc.psum_tensor("ln_ps_rms", [128, TG], F32))

    def row(k):
        return rows[:, k * TG:(k + 1) * TG]

    R_M, R_SD, R_T, R_RMS = 0, 1, 2, 3

    for tg in range(NTG):
        xb = x_col[tg % 2]
        p.dma("sync", lambda e, xb=xb, tg=tg: e.dma_start(out=xb[:], in_=col_ap(io.x_in, tg)),
              reads=[io.x_in], writes=[xb])
        for dc in range(DC):
            p.op("tensor", lambda e, xb=xb, dc=dc: e.matmul(
                ps_sum[:], cst.ones_col[:], xb[:, dc * TG:(dc + 1) * TG],
                start=(dc == 0), stop=(dc == DC - 1)),
                reads=[xb, cst.ones_col], writes=[ps_sum])
        for dc in range(DC):
            p.op("scalar", lambda e, xb=xb, dc=dc: e.activation(
                sqx[:], xb[:, dc * TG:(dc + 1) * TG], AF.Square),
                reads=[xb], writes=[sqx])
            p.op("tensor", lambda e, dc=dc: e.matmul(
                ps_sq[:], cst.ones_col[:], sqx[:], start=(dc == 0), stop=(dc == DC - 1)),
                reads=[sqx, cst.ones_col], writes=[ps_sq])
        p.op("vector", lambda e: e.tensor_scalar(row(R_M), ps_sum[:], 1.0 / D, None, ALU.mult),
             reads=[ps_sum], writes=["r_m"])
        p.op("vector", lambda e: e.tensor_scalar(row(R_T), ps_sq[:], 1.0 / D, None, ALU.mult),
             reads=[ps_sq], writes=["r_t"])
        p.op("vector", lambda e: e.tensor_tensor(row(R_SD), row(R_M), row(R_M), ALU.mult),
             reads=["r_m"], writes=["r_sd"])
        p.op("vector", lambda e: e.tensor_tensor(row(R_T), row(R_T), row(R_SD), ALU.subtract),
             reads=["r_t", "r_sd"], writes=["r_t"])
        p.op("vector", lambda e: e.tensor_scalar(row(R_T), row(R_T), LN_EPS, None, ALU.add),
             reads=["r_t"], writes=["r_t"])
        p.op("scalar", lambda e: e.activation(row(R_SD), row(R_T), AF.Sqrt),
             reads=["r_t"], writes=["r_sd"])
        p.op("vector", lambda e: e.reciprocal(row(R_SD), row(R_SD)),
             reads=["r_sd"], writes=["r_sd"])
        p.op("tensor", lambda e: e.matmul(ps_mbc[:], cst.ones_row[:], row(R_M),
                                          start=True, stop=True),
             reads=["r_m", cst.ones_row], writes=[ps_mbc])
        p.op("tensor", lambda e: e.matmul(ps_rbc[:], cst.ones_row[:], row(R_SD),
                                          start=True, stop=True),
             reads=["r_sd", cst.ones_row], writes=[ps_rbc])
        for dc in range(DC):
            c = slice(dc * TG, (dc + 1) * TG)
            p.op("vector", lambda e, xb=xb, c=c: e.tensor_tensor(
                xn_col[:, c], xb[:, c], ps_mbc[:], ALU.subtract),
                reads=[xb, ps_mbc], writes=[(xn_col)])
            p.op("vector", lambda e, c=c: e.tensor_tensor(
                xn_col[:, c], xn_col[:, c], ps_rbc[:], ALU.mult),
                reads=[xn_col, ps_rbc], writes=[xn_col])
            if block0:
                p.op("scalar", lambda e, c=c: e.activation(sqx[:], xn_col[:, c], AF.Square),
                     reads=[xn_col], writes=[sqx])
                p.op("tensor", lambda e, dc=dc: e.matmul(
                    ps_rms[0:1, :], cst.ones_col[:], sqx[:],
                    start=(dc == 0), stop=(dc == DC - 1)),
                    reads=[sqx, cst.ones_col], writes=["ps_rms_row"])
        if block0:
            p.op("vector", lambda e: e.tensor_scalar(row(R_RMS), ps_rms[0:1, :], 1.0 / D,
                                                     None, ALU.mult),
                 reads=["ps_rms_row"], writes=["r_rms"])
            p.op("vector", lambda e: e.tensor_scalar(row(R_RMS), row(R_RMS), RMS_EPS,
                                                     None, ALU.add),
                 reads=["r_rms"], writes=["r_rms"])
            p.op("scalar", lambda e: e.activation(row(R_RMS), row(R_RMS), AF.Sqrt),
                 reads=["r_rms"], writes=["r_rms"])
            p.op("vector", lambda e: e.reciprocal(row(R_RMS), row(R_RMS)),
                 reads=["r_rms"], writes=["r_rms"])
            p.op("tensor", lambda e: e.matmul(ps_rms[:], cst.ones_row[:], row(R_RMS),
                                              start=True, stop=True),
                 reads=["r_rms", "ps_rms_row", cst.ones_row], writes=[ps_rms])
        of, oh, oxn = out_f[tg % 2], out_h[tg % 2], out_xn[tg % 2]
        for dc in range(DC):
            c = slice(dc * TG, (dc + 1) * TG)
            p.op("vector", lambda e, c=c, oxn=oxn: e.tensor_copy(oxn[:, c], xn_col[:, c]),
                 reads=[xn_col], writes=[oxn])
            if block0:
                p.op("vector", lambda e, xb=xb, c=c, of=of: e.tensor_tensor(
                    of[:, c], xb[:, c], xn_col[:, c], ALU.add),
                    reads=[xb, xn_col], writes=[of])
                p.op("vector", lambda e, c=c, dc=dc: e.tensor_scalar(
                    hsc[:], xn_col[:, c], cst.nw0[:, dc:dc + 1], None, ALU.mult),
                    reads=[xn_col, cst.nw0], writes=[hsc])
                p.op("vector", lambda e, c=c, oh=oh: e.tensor_tensor(
                    oh[:, c], hsc[:], ps_rms[:], ALU.mult),
                    reads=[hsc, ps_rms], writes=[oh])
        p.dma("sync", lambda e, oxn=oxn, tg=tg: e.dma_start(out=col_ap(io.xn, tg), in_=oxn[:]),
              reads=[oxn], writes=[io.xn])
        if block0:
            p.dma("scalar", lambda e, of=of, tg=tg: e.dma_start(out=col_ap(io.xacc, tg), in_=of[:]),
                  reads=[of], writes=[io.xacc])
            p.dma("gpsimd", lambda e, oh=oh, tg=tg: e.dma_start(out=col_ap(io.h, tg), in_=oh[:]),
                  reads=[oh], writes=[io.h])





def emit_attn0(p, ctx, io, cst):
    nc = p.nc
    E = ctx.enter_context
    ctx_all = [E(nc.sbuf_tensor(f"a0_ctx{i}", [128, S], BF16)) for i in range(8)]
    with ExitStack() as hctx:
        EH = hctx.enter_context
        xng = [EH(nc.sbuf_tensor(f"a0_xng{i}", [128, DC * TG], BF16)) for i in range(2)]
        wq_sb = [EH(nc.sbuf_tensor(f"a0_wq{i}", [128, DC * 384], BF16)) for i in range(2)]
        bt_sb = [EH(nc.sbuf_tensor(f"a0_bt{i}", [128, 512], F32)) for i in range(2)]
        bq_bf = EH(nc.sbuf_tensor("a0_bq", [1, NH * 192], BF16))
        qkv_tm = EH(nc.sbuf_tensor("a0_qkvtm", [128, 4 * 384], F32))
        rsc = EH(nc.sbuf_tensor("a0_rsc", [128, 4 * 256], F32))
        qk_rot = EH(nc.sbuf_tensor("a0_qkrot", [128, 4 * 256], BF16))
        qf = [EH(nc.sbuf_tensor(f"a0_qf{i}", [65, S], BF16)) for i in range(2)]
        kf = [EH(nc.sbuf_tensor(f"a0_kf{i}", [65, S], BF16)) for i in range(2)]
        va = [EH(nc.sbuf_tensor(f"a0_va{i}", [128, 32 * 65], BF16)) for i in range(2)]
        qsq = EH(nc.sbuf_tensor("a0_qsq", [64, TG], BF16))
        u65 = EH(nc.sbuf_tensor("a0_u65", [65, TG], F32))
        iot = EH(nc.sbuf_tensor("a0_iot", [65, TG], F32))
        krow = EH(nc.sbuf_tensor("a0_krow", [65, 20], F32))
        e_sb = [EH(nc.sbuf_tensor(f"a0_e{i}", [128, TG], BF16)) for i in range(2)]
        rz = EH(nc.sbuf_tensor("a0_rz", [65, TG], BF16))
        rzf = EH(nc.sbuf_tensor("a0_rzf", [65, TG], F32))
        cx_sb = EH(nc.sbuf_tensor("a0_cx", [128, TG], BF16))
        cxb = EH(nc.sbuf_tensor("a0_cxb", [64, TG], BF16))
        ps_qkv2 = [EH(nc.psum_tensor(f"a0_ps_qkv{i}", [128, 384], F32)) for i in range(2)]
        ps_tp = [EH(nc.psum_tensor("a0_ps_tp0", [64, TG], BF16))] * 2
        ps_st = EH(nc.psum_tensor("a0_ps_st", [128, TG], F32))
        ps_s = [EH(nc.psum_tensor(f"a0_ps_s{i}", [128, TG], F32)) for i in range(2)]
        ps_ctx = EH(nc.psum_tensor("a0_ps_ctx", [65, TG], F32))
        ps_rzb = EH(nc.psum_tensor("a0_ps_rzb", [64, TG], F32))

        p.dma("gpsimd", lambda e: e.dma_start(out=bq_bf[:], in_=io.hbm["b_qkv0"]),
              reads=[io.hbm["b_qkv0"]], writes=[bq_bf])
        p.dma("sync", lambda e: e.dma_start(out=iot[64:65, :], in_=io.hbm["iota_loc"]),
              reads=[io.hbm["iota_loc"]], writes=[iot])
        for i in range(2):
            p.op("vector", lambda e, i=i: e.memset(
                va[i][:].rearrange("p (t c) -> p t c", c=65)[:, :, 64:65], 1.0),
                writes=[va[i]])
            p.op("vector", lambda e, i=i: e.memset(kf[i][64:65, :], 1.0), writes=[kf[i]])

        slopes = (_slopes16() * MAX_BIAS).tolist()

        for hp in range(8):
            wq = wq_sb[hp % 2]
            bt = bt_sb[hp % 2]
            p.dma("gpsimd", lambda e, wq=wq, hp=hp: e.dma_start(
                out=wq[:].rearrange("p (c f) -> p c f", c=DC),
                in_=io.hbm["w_qkv0"].rearrange("(c p) f -> p c f", p=128)[:, :, hp * 384:(hp + 1) * 384]),
                reads=[io.hbm["w_qkv0"]], writes=[wq])
            p.dma("gpsimd", lambda e, bt=bt, hp=hp: e.dma_start(
                out=bt[:], in_=io.hbm["bias_tbl0"][hp]),
                reads=[io.hbm["bias_tbl0"]], writes=[bt])
            for g in range(8):
                xg = xng[g % 2]
                p.dma("sync", lambda e, xg=xg, g=g: e.dma_start(
                    out=xg[:], in_=col_ap(io.xn, g)),
                    reads=[io.xn], writes=[xg])
                for t4 in range(4):
                    ps_qkv = ps_qkv2[t4 % 2]
                    for dc in range(DC):
                        p.op("tensor", lambda e, t4=t4, dc=dc, wq=wq, xg=xg, ps_qkv=ps_qkv: e.matmul(
                            ps_qkv[:], xg[:, dc * TG + t4 * 128:dc * TG + (t4 + 1) * 128],
                            wq[:, dc * 384:(dc + 1) * 384], start=(dc == 0), stop=False),
                            reads=[xg, wq], writes=[ps_qkv])
                    p.op("tensor", lambda e, hp=hp, ps_qkv=ps_qkv: e.matmul(
                        ps_qkv[:], cst.ones_bf[0:1, 0:128], bq_bf[:, hp * 384:(hp + 1) * 384],
                        start=False, stop=True),
                        reads=[cst.ones_bf, bq_bf], writes=[ps_qkv])
                    p.op("vector", lambda e, t4=t4, ps_qkv=ps_qkv: e.tensor_copy(
                        qkv_tm[:, t4 * 384:(t4 + 1) * 384], ps_qkv[:]),
                        reads=[ps_qkv], writes=[qkv_tm])
                q4 = qkv_tm[:].rearrange("p (t h x) -> p t h x", t=4, h=2)
                cosg = cst.cos_tbl[:, g * 128:(g + 1) * 128].rearrange(
                    "p (t j) -> p t j", j=32).unsqueeze(2).broadcast_to([128, 4, 2, 32])
                sing = cst.sin_tbl[:, g * 128:(g + 1) * 128].rearrange(
                    "p (t j) -> p t j", j=32).unsqueeze(2).broadcast_to([128, 4, 2, 32])
                rs = rsc[:].rearrange("p (k t h j) -> p k t h j", k=4, t=4, j=32)
                rot6 = qk_rot[:].rearrange("p (t a h b j) -> p t a h b j", t=4, a=2, h=2, b=2)
                for a, off in ((0, 0), (1, 64)):
                    x1 = q4[:, :, :, off + 0:off + 32]
                    x2 = q4[:, :, :, off + 32:off + 64]
                    eng = "vector" if a == 0 else "gpsimd"
                    p.op(eng, lambda e, x1=x1, c=cosg: e.tensor_tensor(rs[:, 0], x1, c, ALU.mult),
                         reads=[qkv_tm, cst.cos_tbl], writes=["a0rs0"])
                    p.op(eng, lambda e, x2=x2, s=sing: e.tensor_tensor(rs[:, 1], x2, s, ALU.mult),
                         reads=[qkv_tm, cst.sin_tbl], writes=["a0rs1"])
                    p.op(eng, lambda e, a=a: e.tensor_tensor(
                        rot6[:, :, a, :, 0, :], rs[:, 0], rs[:, 1], ALU.subtract),
                        reads=["a0rs0", "a0rs1"], writes=[qk_rot])
                    p.op(eng, lambda e, x2=x2, c=cosg: e.tensor_tensor(rs[:, 2], x2, c, ALU.mult),
                         reads=[qkv_tm, cst.cos_tbl], writes=["a0rs2"])
                    p.op(eng, lambda e, x1=x1, s=sing: e.tensor_tensor(rs[:, 3], x1, s, ALU.mult),
                         reads=[qkv_tm, cst.sin_tbl], writes=["a0rs3"])
                    p.op(eng, lambda e, a=a: e.tensor_tensor(
                        rot6[:, :, a, :, 1, :], rs[:, 2], rs[:, 3], ALU.add),
                        reads=["a0rs2", "a0rs3"], writes=[qk_rot])
                for h2 in range(2):
                    p.op("vector", lambda e, h2=h2, g=g: e.tensor_copy(
                        va[h2][:, g * 4 * 65:(g + 1) * 4 * 65].rearrange(
                            "p (t c) -> p t c", c=65)[:, :, 0:64],
                        q4[:, :, h2, 128:192]),
                        reads=[qkv_tm], writes=[va[h2]])
                for si, (a, h2) in enumerate(((0, 0), (0, 1), (1, 0), (1, 1))):
                    tp = ps_tp[si % 2]
                    for t4 in range(4):
                        col = t4 * 256 + a * 128 + h2 * 64
                        p.op("tensor", lambda e, tp=tp, t4=t4, col=col: e.transpose(
                            tp[0:64, t4 * 128:(t4 + 1) * 128],
                            qk_rot[:, col:col + 64], cst.ident_bf[:]),
                            reads=[qk_rot, cst.ident_bf], writes=[tp])
                    dst = qf[h2] if a == 0 else kf[h2]
                    p.op("vector", lambda e, tp=tp, dst=dst, g=g: e.tensor_copy(
                        dst[0:64, g * TG:(g + 1) * TG], tp[0:64, :]),
                        reads=[tp], writes=[dst])
            for h2 in range(2):
                h = 2 * hp + h2
                for c in range(8):
                    p.op("scalar", lambda e, c=c, h2=h2: e.activation(
                        qsq[:], kf[h2][0:64, c * TG:(c + 1) * TG], AF.Square),
                        reads=[kf[h2]], writes=[qsq])
                    p.op("tensor", lambda e: e.matmul(
                        ps_st[64:65, :], cst.ones_bf[0:64, 0:1], qsq[:],
                        start=True, stop=True, tile_position=(0, 64)),
                        reads=[qsq, cst.ones_bf], writes=[ps_st])
                    p.op("vector", lambda e, c=c: e.reduce_max(
                        krow[64:65, c:c + 1], ps_st[64:65, :], AX.X),
                        reads=[ps_st], writes=[krow])
                p.op("vector", lambda e: e.reduce_max(
                    krow[64:65, 8:9], krow[64:65, 0:8], AX.X), reads=[krow], writes=["a0km"])
                p.op("scalar", lambda e: e.activation(
                    krow[64:65, 9:10], krow[64:65, 8:9], AF.Sqrt),
                    reads=["a0km"], writes=["a0km2"])
                for c in range(8):
                    p.op("scalar", lambda e, c=c, h2=h2: e.activation(
                        qsq[:], qf[h2][0:64, c * TG:(c + 1) * TG], AF.Square),
                        reads=[qf[h2]], writes=[qsq])
                    p.op("tensor", lambda e: e.matmul(
                        ps_st[64:65, :], cst.ones_bf[0:64, 0:1], qsq[:],
                        start=True, stop=True, tile_position=(0, 64)),
                        reads=[qsq, cst.ones_bf], writes=[ps_st])
                    p.op("vector", lambda e, c=c: e.reduce_max(
                        krow[64:65, 10 + c:11 + c], ps_st[64:65, :], AX.X),
                        reads=[ps_st], writes=["a0qm"])
                p.op("vector", lambda e: e.reduce_max(
                    krow[64:65, 2:3], krow[64:65, 10:14], AX.X), reads=["a0qm"], writes=["a0qx"])
                p.op("vector", lambda e: e.reduce_max(
                    krow[64:65, 3:4], krow[64:65, 14:18], AX.X), reads=["a0qm"], writes=["a0qy"])
                p.op("vector", lambda e: e.tensor_tensor(
                    krow[64:65, 4:5], krow[64:65, 2:3], krow[64:65, 3:4], ALU.max),
                    reads=["a0qx", "a0qy"], writes=["a0qz"])
                p.op("scalar", lambda e: e.activation(
                    krow[64:65, 5:6], krow[64:65, 4:5], AF.Sqrt),
                    reads=["a0qz"], writes=["a0qs"])
                p.op("vector", lambda e: e.tensor_tensor(
                    krow[64:65, 6:7], krow[64:65, 5:6], krow[64:65, 9:10], ALU.mult),
                    reads=["a0qs", "a0km2"], writes=["a0bnd"])
                for qb in range(8):
                    p.op("vector", lambda e, h=h, h2=h2, qb=qb: e.scalar_tensor_tensor(
                        qf[h2][64:65, qb * TG:(qb + 1) * TG], iot[64:65, :],
                        -SQ192 * slopes[h], krow[64:65, 6:7].to_broadcast([1, TG]),
                        ALU.mult, ALU.subtract),
                        reads=[iot, "a0bnd"], writes=[qf[h2]])
                for qb in range(8):
                    nkt = min(4 * qb + 4, 32)
                    kt0 = max(0, int((qb * 512 - 120.0 / slopes[h] - 127) // 128))
                    for k in range(kt0, nkt + 1):
                        if k < nkt:
                            kt = k
                            sb = ps_s[kt % 2]
                            eb = e_sb[kt % 2]
                            p.op("tensor", lambda e, sb=sb, h2=h2, kt=kt, qb=qb: e.matmul(
                                sb[:], kf[h2][0:65, kt * 128:(kt + 1) * 128],
                                qf[h2][0:65, qb * TG:(qb + 1) * TG], start=True, stop=True),
                                reads=[kf[h2], qf[h2]], writes=[sb])
                            rel = kt - 4 * qb
                            if 0 <= rel <= 3:
                                p.op("vector", lambda e, sb=sb, rel=rel: e.tensor_tensor(
                                    sb[:], sb[:], cst.masksneg[:, rel * TG:(rel + 1) * TG], ALU.add),
                                    reads=[sb, cst.masksneg], writes=[sb])
                            p.op("scalar", lambda e, sb=sb, eb=eb, h2=h2, kt=kt, qb=qb, bt=bt:
                                 e.activation(eb[:], sb[:], AF.Exp, scale=ISQ192,
                                              bias=bt[:, h2 * 256 + kt * 8 + qb:h2 * 256 + kt * 8 + qb + 1]),
                                 reads=[sb, bt], writes=[eb])
                        if k > kt0:
                            kt = k - 1
                            eb = e_sb[kt % 2]
                            p.op("tensor", lambda e, eb=eb, h2=h2, kt=kt, first=(kt == kt0),
                                 last=(kt == nkt - 1):
                                 e.matmul(ps_ctx[:], va[h2][:, kt * 65:(kt + 1) * 65], eb[:],
                                          start=first, stop=last),
                                 reads=[eb, va[h2]], writes=[ps_ctx])
                    p.op("vector", lambda e: e.reciprocal(rzf[64:65, :], ps_ctx[64:65, :]),
                         reads=[ps_ctx], writes=[rzf])
                    p.op("vector", lambda e: e.tensor_copy(rz[64:65, :], rzf[64:65, :]),
                         reads=[rzf], writes=[rz])
                    p.op("tensor", lambda e: e.matmul(
                        ps_rzb[:], cst.ones_bf[64:65, 0:64], rz[64:65, :],
                        start=True, stop=True),
                        reads=[rz, cst.ones_bf], writes=[ps_rzb])
                    p.op("vector", lambda e: e.tensor_copy(cx_sb[0:64, :], ps_ctx[0:64, :]),
                         reads=[ps_ctx], writes=[cx_sb])
                    if h2 == 0:
                        p.op("vector", lambda e, hp=hp, qb=qb: e.tensor_tensor(
                            ctx_all[hp][0:64, qb * TG:(qb + 1) * TG],
                            cx_sb[0:64, :], ps_rzb[:], ALU.mult),
                            reads=[cx_sb, ps_rzb], writes=[ctx_all[hp]])
                    else:
                        p.op("vector", lambda e: e.tensor_tensor(
                            cxb[0:64, :], cx_sb[0:64, :], ps_rzb[:], ALU.mult),
                            reads=[cx_sb, ps_rzb], writes=[cxb])
                        p.dma("sync", lambda e, hp=hp, qb=qb: e.dma_start(
                            out=ctx_all[hp][64:128, qb * TG:(qb + 1) * TG], in_=cxb[0:64, :]),
                            reads=[cxb], writes=[ctx_all[hp]])
    with ExitStack() as octx:
        EO = octx.enter_context
        wo_sb = EO(nc.sbuf_tensor("a0_wo", [128, 8 * 1024], BF16))
        ot = [EO(nc.sbuf_tensor(f"a0_ot{i}", [128, TG], F32)) for i in range(2)]
        ps_o = [EO(nc.psum_tensor(f"a0_ps_o{i}", [128, TG], F32)) for i in range(2)]
        p.dma("gpsimd", lambda e: e.dma_start(
            out=wo_sb[:].rearrange("p (c f) -> p c f", c=8),
            in_=io.hbm["w_out0"].rearrange("(c p) f -> p c f", p=128)),
            reads=[io.hbm["w_out0"]], writes=[wo_sb])
        for mt in range(8):
            for tg in range(8):
                sl = (mt * 8 + tg) % 2
                for ec in range(8):
                    p.op("tensor", lambda e, sl=sl, ec=ec, mt=mt, tg=tg: e.matmul(
                        ps_o[sl][:], wo_sb[:, ec * 1024 + mt * 128:ec * 1024 + (mt + 1) * 128],
                        ctx_all[ec][:, tg * TG:(tg + 1) * TG],
                        start=(ec == 0), stop=(ec == 7)),
                        reads=[wo_sb, ctx_all[ec]], writes=[ps_o[sl]])
                p.op("vector", lambda e, sl=sl, mt=mt: e.tensor_scalar(
                    ot[sl][:], ps_o[sl][:], cst.b_out0[:, mt:mt + 1], None, ALU.add),
                    reads=[ps_o[sl], cst.b_out0], writes=[ot[sl]])
                p.dma("sync", lambda e, sl=sl, mt=mt, tg=tg: e.dma_start(
                    out=io.attnout.rearrange("(c p) t -> p c t", p=128)[:, mt, tg * TG:(tg + 1) * TG],
                    in_=ot[sl][:]),
                    reads=[ot[sl]], writes=[io.attnout])



def emit_ffn(p, ctx, io, cst, block0):
    nc = p.nc
    E = ctx.enter_context
    cT = E(nc.sbuf_tensor("f_cT", [8, S], BF16))
    macc = E(nc.sbuf_tensor("f_macc", [128, DC * 2048], F32))
    gw_sb = E(nc.sbuf_tensor("f_gw", [128, DC * 8], BF16))
    lgt = E(nc.sbuf_tensor("f_lgt", [128, 16 * 8], F32))
    cbf = E(nc.sbuf_tensor("f_cbf", [128, 8], BF16))
    h_roll = [E(nc.sbuf_tensor(f"f_h{i}", [128, DC * TG], BF16)) for i in range(2)]
    if block0:
        gb_bf = E(nc.sbuf_tensor("f_gb", [1, 8], BF16))
        b2_sb = E(nc.sbuf_tensor("f_b2", [8, 1024], BF16))
        b1t_sb = E(nc.sbuf_tensor("f_b1t", [128, 16], F32))
        gsb = E(nc.sbuf_tensor("f_gsb", [128, TG], F32))
        sig = E(nc.sbuf_tensor("f_sig", [128, TG], BF16))
        gcb = E(nc.sbuf_tensor("f_gcb", [128, TG], BF16))
        lsb = E(nc.sbuf_tensor("f_lsb", [128, TG], F32))
        lbf = E(nc.sbuf_tensor("f_lbf", [128, TG], BF16))
        gs2 = E(nc.sbuf_tensor("f_gs2", [128, TG], BF16))
    else:
        scbc = E(nc.sbuf_tensor("f_scbc", [128, TG], BF16))
        y1 = E(nc.sbuf_tensor("f_y1", [128, TG], BF16))
    nact = 8 if block0 else 16
    act = E(nc.sbuf_tensor("f_act", [128, nact * TG], BF16))
    w1_sb = [E(nc.sbuf_tensor(f"f_w1{i}", [128, DC * 2048], BF16)) for i in range(2)]
    w2cols = 8 * 1024 if block0 else 16 * 1024
    w2_sb = E(nc.sbuf_tensor("f_w2", [128, w2cols], BF16))
    att_view = h_roll[0][:].bitcast(F32)
    att_roll = [att_view, att_view]

    gw_src = "gate_w0" if block0 else "wg1"
    rhs_hbm = io.h
    p.dma("gpsimd", lambda e: e.dma_start(
        out=gw_sb[:].rearrange("p (c f) -> p c f", c=DC),
        in_=io.hbm[gw_src].rearrange("(c p) f -> p c f", p=128)),
        reads=[io.hbm[gw_src]], writes=[gw_sb])
    if block0:
        p.dma("gpsimd", lambda e: e.dma_start(out=gb_bf[:], in_=io.hbm["gate_b0"]),
              reads=[io.hbm["gate_b0"]], writes=[gb_bf])
        p.dma("gpsimd", lambda e: e.dma_start(out=b2_sb[:], in_=io.hbm["b2_0"]),
              reads=[io.hbm["b2_0"]], writes=[b2_sb])

    def gate_tile(tg, q, sl, ps_g, ps_ct):
        hb = h_roll[sl]
        for dc in range(DC):
            p.op("tensor", lambda e, dc=dc, q=q, hb=hb: e.matmul(
                ps_g[:], hb[:, dc * TG + q * 128:dc * TG + (q + 1) * 128],
                gw_sb[:, dc * 8:(dc + 1) * 8], start=(dc == 0), stop=not block0 and dc == DC - 1),
                reads=[hb, gw_sb], writes=[ps_g])
        if block0:
            p.op("tensor", lambda e: e.matmul(
                ps_g[:], cst.ones_bf[0:1, 0:128], gb_bf[:], start=False, stop=True),
                reads=[cst.ones_bf, gb_bf], writes=[ps_g])
        L = lambda k: lgt[:, k * 8:(k + 1) * 8]
        M = lambda k: lgt[:, 64 + k]
        p.op("vector", lambda e: e.tensor_copy(L(0), ps_g[:]), reads=[ps_g], writes=["fL0"])
        p.op("vector", lambda e: e.reduce_max(lgt[:, 64:65], L(0), AX.X),
             reads=["fL0"], writes=["fm1"])
        p.op("vector", lambda e: e.tensor_tensor(
            L(1), L(0), lgt[:, 64:65].to_broadcast([128, 8]), ALU.is_ge),
             reads=["fL0", "fm1"], writes=["fL1"])
        p.op("vector", lambda e: e.scalar_tensor_tensor(L(2), L(1), -1e30, L(0),
                                                        ALU.mult, ALU.add),
             reads=["fL1", "fL0"], writes=["fL2"])
        p.op("vector", lambda e: e.reduce_max(lgt[:, 65:66], L(2), AX.X),
             reads=["fL2"], writes=["fm2"])
        p.op("vector", lambda e: e.tensor_tensor(
            L(3), L(2), lgt[:, 65:66].to_broadcast([128, 8]), ALU.is_ge),
             reads=["fL2", "fm2"], writes=["fL3"])
        if block0:
            p.op("vector", lambda e: e.tensor_tensor(lgt[:, 66:67], lgt[:, 64:65],
                                                     lgt[:, 65:66], ALU.subtract),
                 reads=["fm1", "fm2"], writes=["fw1"])
            p.op("scalar", lambda e: e.activation(lgt[:, 66:67], lgt[:, 66:67], AF.Sigmoid),
                 reads=["fw1"], writes=["fw1"])
            p.op("vector", lambda e: e.tensor_scalar(lgt[:, 67:68], lgt[:, 66:67], -1.0, 1.0,
                                                     ALU.mult, ALU.add),
                 reads=["fw1"], writes=["fw2"])
            p.op("vector", lambda e: e.tensor_tensor(
                L(4), L(1), lgt[:, 66:67].to_broadcast([128, 8]), ALU.mult),
                 reads=["fL1", "fw1"], writes=["fL4"])
            p.op("vector", lambda e: e.tensor_tensor(
                L(5), L(3), lgt[:, 67:68].to_broadcast([128, 8]), ALU.mult),
                 reads=["fL3", "fw2"], writes=["fL5"])
            p.op("vector", lambda e: e.tensor_tensor(cbf[:], L(5), L(4), ALU.add),
                 reads=["fL5", "fL4"], writes=[cbf])
        else:
            p.op("vector", lambda e: e.tensor_scalar(lgt[:, 67:68], lgt[:, 64:65], -1.0,
                                                     None, ALU.mult),
                 reads=["fm1"], writes=["fneg"])
            p.op("scalar", lambda e: e.activation(L(4), L(0), AF.Exp, bias=lgt[:, 67:68],
                                                  accum_out=lgt[:, 68:69]),
                 reads=["fL0", "fneg"], writes=["fL4", "fZ"])
            p.op("vector", lambda e: e.tensor_tensor(L(3), L(3), L(1), ALU.add),
                 reads=["fL3", "fL1"], writes=["fL3"])
            p.op("vector", lambda e: e.tensor_tensor(L(5), L(4), L(3), ALU.mult),
                 reads=["fL4", "fL3"], writes=["fL5"])
            p.op("vector", lambda e: e.tensor_reduce(lgt[:, 66:67], L(5), AX.X, ALU.add),
                 reads=["fL5"], writes=["fden"])
            p.op("vector", lambda e: e.scalar_tensor_tensor(
                lgt[:, 66:67], lgt[:, 68:69], 1e-8, lgt[:, 66:67], ALU.mult, ALU.add),
                reads=["fZ", "fden"], writes=["fden"])
            p.op("vector", lambda e: e.reciprocal(lgt[:, 67:68], lgt[:, 66:67]),
                 reads=["fden"], writes=["frec"])
            p.op("vector", lambda e: e.tensor_tensor(
                L(6), L(5), lgt[:, 67:68].to_broadcast([128, 8]), ALU.mult),
                 reads=["fL5", "frec"], writes=["fc"])
            p.op("scalar", lambda e: e.activation(cbf[:], L(6), AF.Sqrt),
                 reads=["fc"], writes=[cbf])
        p.op("tensor", lambda e: e.transpose(ps_ct[:], cbf[:], cst.ident_bf[:]),
             reads=[cbf, cst.ident_bf], writes=[ps_ct])
        p.op("vector", lambda e, tg=tg, q=q: e.tensor_copy(
            cT[0:8, tg * TG + q * 128:tg * TG + (q + 1) * 128], ps_ct[0:8, :]),
            reads=[ps_ct], writes=[cT])

    for th in range(2):
        base_src = io.xacc
        p.dma("sync", lambda e, th=th: e.dma_start(
            out=macc[:].rearrange("p (c t) -> p c t", c=DC),
            in_=base_src.rearrange("(c p) t -> p c t", p=128)[:, :, th * 2048:(th + 1) * 2048]),
            reads=[base_src], writes=[macc])
        for dc in range(DC):
            ar = att_roll[0]
            p.dma("sync", lambda e, dc=dc, th=th, ar=ar: e.dma_start(
                out=ar,
                in_=io.attnout.rearrange("(c p) t -> p c t", p=128)[:, dc, th * 2048:(th + 1) * 2048]),
                reads=[io.attnout], writes=[h_roll[0]])
            p.op("vector", lambda e, dc=dc, ar=ar: e.tensor_tensor(
                macc[:, dc * 2048:(dc + 1) * 2048], macc[:, dc * 2048:(dc + 1) * 2048],
                ar, ALU.add),
                reads=[h_roll[0], macc], writes=[macc])
        with ExitStack() as gctx:
            ps_g = gctx.enter_context(nc.psum_tensor("f_ps_g", [128, 8], F32))
            ps_ct = gctx.enter_context(nc.psum_tensor("f_ps_ct", [8, 128], BF16))
            for tg4 in range(4):
                tg = th * 4 + tg4
                sl = tg4 % 2
                p.dma("scalar", lambda e, tg=tg, sl=sl: e.dma_start(
                    out=h_roll[sl][:], in_=col_ap(rhs_hbm, tg)),
                    reads=[rhs_hbm], writes=[h_roll[sl]])
                for q in range(4):
                    gate_tile(tg, q, sl, ps_g, ps_ct)
            p.barrier()
        ectx = ExitStack()
        ps_cbc = ectx.enter_context(nc.psum_tensor("f_ps_cbc", [128, TG], F32))
        ps_m1 = [ectx.enter_context(nc.psum_tensor(f"f_ps_m1_{i}", [128, TG], F32))
                 for i in range(4)]
        ps_y = [ectx.enter_context(nc.psum_tensor(f"f_ps_y{i}", [128, TG], F32))
                for i in range(2)]
        for e_i in range(8):
            w1 = w1_sb[e_i % 2]
            wsrc = "w1t0" if block0 else "w1_1"
            w2src = "w2t0" if block0 else "w2_1"
            p.dma("gpsimd", lambda e, w1=w1, e_i=e_i, wsrc=wsrc: e.dma_start(
                out=w1[:].rearrange("p (c f) -> p c f", c=DC),
                in_=io.hbm[wsrc][e_i].rearrange("(c p) f -> p c f", p=128)),
                reads=[io.hbm[wsrc]], writes=[w1])
            nw2c = 8 if block0 else 16
            p.dma("gpsimd", lambda e, e_i=e_i, w2src=w2src, nw2c=nw2c: e.dma_start(
                out=w2_sb[:].rearrange("p (c f) -> p c f", c=nw2c),
                in_=io.hbm[w2src][e_i].rearrange("(c p) f -> p c f", p=128)),
                reads=[io.hbm[w2src]], writes=[w2_sb])
            if block0:
                p.dma("gpsimd", lambda e, e_i=e_i: e.dma_start(
                    out=b1t_sb[:], in_=io.hbm["b1t0"][e_i]),
                    reads=[io.hbm["b1t0"]], writes=[b1t_sb])
            for tg4 in range(4):
                tg = th * 4 + tg4
                tsl = tg * TG
                sl = tg4 % 2
                hb = h_roll[sl]
                p.dma("scalar", lambda e, tg=tg, hb=hb: e.dma_start(
                    out=hb[:], in_=col_ap(rhs_hbm, tg)),
                    reads=[rhs_hbm], writes=[hb])
                p.op("tensor", lambda e, e_i=e_i, tsl=tsl: e.matmul(
                    ps_cbc[:], cst.sel8[0:8, e_i * 128:(e_i + 1) * 128],
                    cT[0:8, tsl:tsl + TG], start=True, stop=True),
                    reads=[cst.sel8, cT], writes=[ps_cbc])
                if not block0:
                    p.op("vector", lambda e: e.tensor_copy(scbc[:], ps_cbc[:]),
                         reads=[ps_cbc], writes=[scbc])
                for ic in range(nact):
                    if block0:
                        ps_a = ps_m1[(ic % 2) * 2]
                        ps_b = ps_m1[(ic % 2) * 2 + 1]
                        for dc in range(DC):
                            p.op("tensor", lambda e, dc=dc, ic=ic, w1=w1, hb=hb, ps_a=ps_a: e.matmul(
                                ps_a[:], w1[:, dc * 2048 + ic * 128:dc * 2048 + (ic + 1) * 128],
                                hb[:, dc * TG:(dc + 1) * TG], start=(dc == 0), stop=(dc == DC - 1)),
                                reads=[w1, hb], writes=[ps_a])
                        for dc in range(DC):
                            p.op("tensor", lambda e, dc=dc, ic=ic, w1=w1, hb=hb, ps_b=ps_b: e.matmul(
                                ps_b[:], w1[:, dc * 2048 + 1024 + ic * 128:dc * 2048 + 1024 + (ic + 1) * 128],
                                hb[:, dc * TG:(dc + 1) * TG], start=(dc == 0), stop=(dc == DC - 1)),
                                reads=[w1, hb], writes=[ps_b])
                        p.op("vector", lambda e, ic=ic, ps_a=ps_a: e.tensor_scalar(
                            gsb[:], ps_a[:], b1t_sb[:, ic:ic + 1], 7.0, ALU.add, ALU.min),
                            reads=[ps_a, b1t_sb], writes=[gsb])
                        p.op("scalar", lambda e: e.activation(sig[:], gsb[:], AF.Sigmoid,
                                                              scale=ALPHA),
                             reads=[gsb], writes=[sig])
                        p.op("vector", lambda e: e.tensor_tensor(gcb[:], gsb[:], ps_cbc[:],
                                                                 ALU.mult),
                             reads=[gsb, ps_cbc], writes=[gcb])
                        p.op("vector", lambda e, ic=ic, ps_b=ps_b: e.tensor_scalar(
                            lsb[:], ps_b[:], b1t_sb[:, 8 + ic:9 + ic], 7.0, ALU.add, ALU.min),
                            reads=[ps_b, b1t_sb], writes=[lsb])
                        p.op("vector", lambda e: e.tensor_scalar(
                            lbf[:], lsb[:], -7.0, 1.0, ALU.max, ALU.add),
                            reads=[lsb], writes=[lbf])
                        p.op("vector", lambda e: e.tensor_tensor(gs2[:], gcb[:], sig[:], ALU.mult),
                             reads=[gcb, sig], writes=[gs2])
                        p.op("vector", lambda e, ic=ic: e.tensor_tensor(
                            act[:, ic * TG:(ic + 1) * TG], gs2[:], lbf[:], ALU.mult),
                            reads=[gs2, lbf], writes=[act])
                    else:
                        pst = ps_m1[ic % 4]
                        for dc in range(DC):
                            p.op("tensor", lambda e, dc=dc, ic=ic, w1=w1, hb=hb, pst=pst: e.matmul(
                                pst[:], w1[:, dc * 2048 + ic * 128:dc * 2048 + (ic + 1) * 128],
                                hb[:, dc * TG:(dc + 1) * TG], start=(dc == 0), stop=(dc == DC - 1)),
                                reads=[w1, hb], writes=[pst])
                        p.op("vector", lambda e, pst=pst: e.tensor_tensor(
                            y1[:], pst[:], scbc[:], ALU.mult),
                            reads=[pst, scbc], writes=[y1])
                        p.op("vector", lambda e, ic=ic: e.scalar_tensor_tensor(
                            act[:, ic * TG:(ic + 1) * TG], y1[:], 0.0, y1[:],
                            ALU.max, ALU.mult),
                            reads=[y1], writes=[act])
                for dt in range(8):
                    py = ps_y[dt % 2]
                    first = True
                    if block0 and e_i == 0:
                        p.op("tensor", lambda e, dt=dt, tsl=tsl, py=py: e.matmul(
                            py[:], b2_sb[0:8, dt * 128:(dt + 1) * 128], cT[0:8, tsl:tsl + TG],
                            start=True, stop=False),
                            reads=[b2_sb, cT], writes=[py])
                        first = False
                    for ic in range(nact):
                        p.op("tensor", lambda e, dt=dt, ic=ic, py=py, first=(first and ic == 0),
                             last=(ic == nact - 1): e.matmul(
                            py[:], w2_sb[:, ic * 1024 + dt * 128:ic * 1024 + (dt + 1) * 128],
                            act[:, ic * TG:(ic + 1) * TG], start=first, stop=last),
                            reads=[w2_sb, act], writes=[py])
                    p.op("vector", lambda e, dt=dt, tg4=tg4, py=py: e.tensor_tensor(
                        macc[:, dt * 2048 + tg4 * TG:dt * 2048 + (tg4 + 1) * TG],
                        macc[:, dt * 2048 + tg4 * TG:dt * 2048 + (tg4 + 1) * TG],
                        py[:], ALU.add),
                        reads=[py, macc], writes=[macc])
        if getattr(io, "dbg_ct", None) is not None and th == 1:
            p.dma("gpsimd", lambda e: e.dma_start(out=io.dbg_ct, in_=cT[0:8, :]),
                  reads=[cT], writes=[io.dbg_ct])
        if getattr(io, "dbg_lg", None) is not None and th == 1:
            p.dma("gpsimd", lambda e: e.dma_start(out=io.dbg_lg, in_=lgt[:]),
                  reads=["fL0", "fL1", "fL2", "fL3", "fL4", "fm1", "fm2", "fw1", "fw2"],
                  writes=[io.dbg_lg])
        p.dma("sync", lambda e, th=th: e.dma_start(
            out=io.x_out.rearrange("(c p) t -> p c t", p=128)[:, :, th * 2048:(th + 1) * 2048],
            in_=macc[:].rearrange("p (c t) -> p c t", c=DC)),
            reads=[macc], writes=[io.x_out])
        p.barrier()
        ectx.close()
    p.barrier()



def emit_attn1(p, ctx, io, cst):
    nc = p.nc
    E = ctx.enter_context
    ctx_all = [E(nc.sbuf_tensor(f"a1_ctx{i}", [128, S], BF16)) for i in range(8)]
    with ExitStack() as hctx:
        EH = hctx.enter_context
        wqk_sb = EH(nc.sbuf_tensor("a1_wqk", [128, DC * 16 * 128], BF16))
        wv_sb = EH(nc.sbuf_tensor("a1_wv", [128, DC * 16 * 64], BF16))
        xsl = [EH(nc.sbuf_tensor(f"a1_xsl{i}", [128, DC * 256], BF16)) for i in range(2)]
        qf = [EH(nc.sbuf_tensor(f"a1_qf{i}", [65, S], BF16)) for i in range(2)]
        kf = [EH(nc.sbuf_tensor(f"a1_kf{i}", [65, S], BF16)) for i in range(2)]
        vfm = EH(nc.sbuf_tensor("a1_vfm", [64, S], BF16))
        stage = EH(nc.sbuf_tensor("a1_stage", [128, S], BF16))
        va = [EH(nc.sbuf_tensor(f"a1_va{i}", [128, 32 * 65], BF16)) for i in range(2)]
        qsq = EH(nc.sbuf_tensor("a1_qsq", [64, TG], BF16))
        u65 = EH(nc.sbuf_tensor("a1_u65", [65, TG], F32))
        krow = EH(nc.sbuf_tensor("a1_krow", [65, 20], F32))
        e_sb = [EH(nc.sbuf_tensor(f"a1_e{i}", [128, 2 * TG], BF16)) for i in range(2)]
        rz = EH(nc.sbuf_tensor("a1_rz", [65, TG], BF16))
        rzf = EH(nc.sbuf_tensor("a1_rzf", [65, TG], F32))
        cx_sb = EH(nc.sbuf_tensor("a1_cx", [128, TG], BF16))
        cxb = EH(nc.sbuf_tensor("a1_cxb", [64, TG], BF16))
        ps_p = EH(nc.psum_tensor("a1_ps_p", [128, 256], F32))
        ps_tp = [EH(nc.psum_tensor("a1_ps_tp0", [128, 256], BF16))] * 2
        ps_misc = EH(nc.psum_tensor("a1_ps_misc", [128, TG], F32))
        ps_st = ps_misc
        ps_s = [EH(nc.psum_tensor(f"a1_ps_s{i}", [128, 2 * TG], F32)) for i in range(2)]
        ps_ctx = EH(nc.psum_tensor("a1_ps_ctx", [65, TG], F32))
        ps_rzb = ps_misc

        p.dma("gpsimd", lambda e: e.dma_start(
            out=wqk_sb[:].rearrange("p (c f) -> p c f", c=DC),
            in_=io.hbm["wqk1"].rearrange("(c p) f -> p c f", p=128)),
            reads=[io.hbm["wqk1"]], writes=[wqk_sb])
        p.dma("gpsimd", lambda e: e.dma_start(
            out=wv_sb[:].rearrange("p (c f) -> p c f", c=DC),
            in_=io.hbm["wv1"].rearrange("(c p) f -> p c f", p=128)),
            reads=[io.hbm["wv1"]], writes=[wv_sb])
        for i in range(2):
            p.op("vector", lambda e, i=i: e.memset(
                va[i][:].rearrange("p (t c) -> p t c", c=65)[:, :, 64:65], 1.0),
                writes=[va[i]])
            p.op("vector", lambda e, i=i: e.memset(kf[i][64:65, :], 1.0), writes=[kf[i]])

        for h in range(16):
            h2 = h % 2
            xs = xsl[h2]
            hp = h // 2
            p.dma("sync", lambda e, xs=xs, h=h: e.dma_start(
                out=xs[:].rearrange("p (c t) -> p c t", c=DC),
                in_=io.xn.rearrange("(c p) t -> p c t", p=128)[:, :, h * 256:(h + 1) * 256]),
                reads=[io.xn], writes=[xs])
            for r in range(16):
                for dc in range(DC):
                    p.op("tensor", lambda e, dc=dc, r=r, xs=xs: e.matmul(
                        ps_p[:], wqk_sb[:, dc * 2048 + r * 128:dc * 2048 + (r + 1) * 128],
                        xs[:, dc * 256:(dc + 1) * 256], start=(dc == 0), stop=(dc == DC - 1)),
                        reads=[wqk_sb, xs], writes=[ps_p])
                p.op("vector", lambda e, r=r, h2=h2: e.tensor_scalar(
                    qf[h2][0:64, r * 256:(r + 1) * 256], ps_p[0:64, :],
                    cst.bqk1[0:64, r:r + 1], None, ALU.add),
                    reads=[ps_p, cst.bqk1], writes=[qf[h2]])
                p.op("vector", lambda e, r=r: e.tensor_scalar(
                    stage[64:128, r * 256:(r + 1) * 256], ps_p[64:128, :],
                    cst.bqk1[64:128, r:r + 1], None, ALU.add),
                    reads=[ps_p, cst.bqk1], writes=[stage])
            p.dma("sync", lambda e, h2=h2: e.dma_start(
                out=kf[h2][0:64, :], in_=stage[64:128, :]),
                reads=[stage], writes=[kf[h2]])
            for rp in range(8):
                for dc in range(DC):
                    p.op("tensor", lambda e, dc=dc, rp=rp, xs=xs: e.matmul(
                        ps_p[:], wv_sb[:, dc * 1024 + rp * 128:dc * 1024 + (rp + 1) * 128],
                        xs[:, dc * 256:(dc + 1) * 256], start=(dc == 0), stop=(dc == DC - 1)),
                        reads=[wv_sb, xs], writes=[ps_p])
                p.op("vector", lambda e, rp=rp: e.tensor_scalar(
                    vfm[0:64, (2 * rp) * 256:(2 * rp + 1) * 256], ps_p[0:64, :],
                    cst.bv1[0:64, rp:rp + 1], None, ALU.add),
                    reads=[ps_p, cst.bv1], writes=[vfm])
                p.op("vector", lambda e, rp=rp: e.tensor_scalar(
                    stage[64:128, rp * 256:(rp + 1) * 256], ps_p[64:128, :],
                    cst.bv1[64:128, rp:rp + 1], None, ALU.add),
                    reads=[ps_p, cst.bv1], writes=[stage])
            p.dma("sync", lambda e: e.dma_start(
                out=vfm[0:64, :].rearrange("p (r two j) -> p r two j", two=2, j=256)[:, :, 1, :],
                in_=stage[64:128, 0:2048]),
                reads=[stage], writes=[vfm])
            for g in range(8):
                tp = ps_tp[g % 2]
                for t4 in range(4):
                    tt = g * 4 + t4
                    p.op("tensor", lambda e, tp=tp, t4=t4, tt=tt: e.transpose(
                        tp[:, t4 * 64:(t4 + 1) * 64],
                        vfm[0:64, tt * 128:(tt + 1) * 128], cst.ident_bf[0:64, 0:64]),
                        reads=[vfm, cst.ident_bf], writes=[tp])
                p.op("vector", lambda e, tp=tp, g=g, h2=h2: e.tensor_copy(
                    va[h2][:, g * 4 * 65:(g + 1) * 4 * 65].rearrange(
                        "p (t c) -> p t c", c=65)[:, :, 0:64],
                    tp[:].rearrange("p (t c) -> p t c", c=64)),
                    reads=[tp], writes=[va[h2]])
            for c in range(8):
                p.op("scalar", lambda e, c=c, h2=h2: e.activation(
                    qsq[:], kf[h2][0:64, c * TG:(c + 1) * TG], AF.Square),
                    reads=[kf[h2]], writes=[qsq])
                p.op("tensor", lambda e: e.matmul(
                    ps_st[64:65, :], cst.ones_bf[0:64, 0:1], qsq[:],
                    start=True, stop=True, tile_position=(0, 64)),
                    reads=[qsq, cst.ones_bf], writes=[ps_st])
                p.op("vector", lambda e, c=c: e.reduce_max(
                    krow[64:65, c:c + 1], ps_st[64:65, :], AX.X),
                    reads=[ps_st], writes=[krow])
            p.op("vector", lambda e: e.reduce_max(
                krow[64:65, 8:9], krow[64:65, 0:8], AX.X), reads=[krow], writes=["a1km"])
            p.op("scalar", lambda e: e.activation(
                krow[64:65, 9:10], krow[64:65, 8:9], AF.Sqrt),
                reads=["a1km"], writes=["a1km2"])
            for c in range(8):
                p.op("scalar", lambda e, c=c, h2=h2: e.activation(
                    qsq[:], qf[h2][0:64, c * TG:(c + 1) * TG], AF.Square),
                    reads=[qf[h2]], writes=[qsq])
                p.op("tensor", lambda e: e.matmul(
                    ps_st[64:65, :], cst.ones_bf[0:64, 0:1], qsq[:],
                    start=True, stop=True, tile_position=(0, 64)),
                    reads=[qsq, cst.ones_bf], writes=[ps_st])
                p.op("vector", lambda e, c=c: e.reduce_max(
                    krow[64:65, 10 + c:11 + c], ps_st[64:65, :], AX.X),
                    reads=[ps_st], writes=["a1qm"])
            p.op("vector", lambda e: e.reduce_max(
                krow[64:65, 2:3], krow[64:65, 10:14], AX.X), reads=["a1qm"], writes=["a1qx"])
            p.op("vector", lambda e: e.reduce_max(
                krow[64:65, 3:4], krow[64:65, 14:18], AX.X), reads=["a1qm"], writes=["a1qy"])
            p.op("vector", lambda e: e.tensor_tensor(
                krow[64:65, 4:5], krow[64:65, 2:3], krow[64:65, 3:4], ALU.max),
                reads=["a1qx", "a1qy"], writes=["a1qz"])
            p.op("scalar", lambda e: e.activation(
                krow[64:65, 5:6], krow[64:65, 4:5], AF.Sqrt),
                reads=["a1qz"], writes=["a1qs"])
            p.op("vector", lambda e: e.scalar_tensor_tensor(
                krow[64:65, 6:7], krow[64:65, 5:6], -1.0, krow[64:65, 9:10],
                ALU.mult, ALU.mult),
                reads=["a1qs", "a1km2"], writes=["a1bnd"])
            for qb in range(8):
                p.op("vector", lambda e, h2=h2, qb=qb: e.tensor_copy(
                    qf[h2][64:65, qb * TG:(qb + 1) * TG],
                    krow[64:65, 6:7].to_broadcast([1, TG])),
                    reads=["a1bnd"], writes=[qf[h2]])
                for k in range(17):
                    if k < 16:
                        kp = k
                        sb = ps_s[kp % 2]
                        eb = e_sb[kp % 2]
                        for half in range(2):
                            kt = 2 * kp + half
                            p.op("tensor", lambda e, sb=sb, h2=h2, kt=kt, qb=qb, half=half: e.matmul(
                                sb[:, half * TG:(half + 1) * TG],
                                kf[h2][0:65, kt * 128:(kt + 1) * 128],
                                qf[h2][0:65, qb * TG:(qb + 1) * TG], start=True, stop=True),
                                reads=[kf[h2], qf[h2]], writes=[sb])
                        p.op("scalar", lambda e, sb=sb, eb=eb: e.activation(
                            eb[:], sb[:], AF.Exp, scale=0.125),
                            reads=[sb], writes=[eb])
                    if k > 0:
                        kp = k - 1
                        eb = e_sb[kp % 2]
                        for half in range(2):
                            kt = 2 * kp + half
                            p.op("tensor", lambda e, eb=eb, h2=h2, kt=kt, half=half,
                                 last=(kt == 31): e.matmul(
                                ps_ctx[:], va[h2][:, kt * 65:(kt + 1) * 65],
                                eb[:, half * TG:(half + 1) * TG],
                                start=(kt == 0), stop=last),
                                reads=[eb, va[h2]], writes=[ps_ctx])
                p.op("vector", lambda e: e.reciprocal(rzf[64:65, :], ps_ctx[64:65, :]),
                     reads=[ps_ctx], writes=[rzf])
                p.op("vector", lambda e: e.tensor_copy(rz[64:65, :], rzf[64:65, :]),
                     reads=[rzf], writes=[rz])
                p.op("tensor", lambda e: e.matmul(
                    ps_rzb[0:64, :], cst.ones_bf[64:65, 0:64], rz[64:65, :],
                    start=True, stop=True),
                    reads=[rz, cst.ones_bf], writes=[ps_rzb])
                p.op("vector", lambda e: e.tensor_copy(cx_sb[0:64, :], ps_ctx[0:64, :]),
                     reads=[ps_ctx], writes=[cx_sb])
                if h2 == 0:
                    p.op("vector", lambda e, hp=hp, qb=qb: e.tensor_tensor(
                        ctx_all[hp][0:64, qb * TG:(qb + 1) * TG],
                        cx_sb[0:64, :], ps_rzb[0:64, :], ALU.mult),
                        reads=[cx_sb, ps_rzb], writes=[ctx_all[hp]])
                else:
                    p.op("vector", lambda e: e.tensor_tensor(
                        cxb[0:64, :], cx_sb[0:64, :], ps_rzb[0:64, :], ALU.mult),
                        reads=[cx_sb, ps_rzb], writes=[cxb])
                    p.dma("sync", lambda e, hp=hp, qb=qb: e.dma_start(
                        out=ctx_all[hp][64:128, qb * TG:(qb + 1) * TG], in_=cxb[0:64, :]),
                        reads=[cxb], writes=[ctx_all[hp]])
        p.barrier()
    with ExitStack() as octx:
        EO = octx.enter_context
        wo_sb = EO(nc.sbuf_tensor("a1_wo", [128, 8 * 1024], BF16))
        ot = [EO(nc.sbuf_tensor(f"a1_ot{i}", [128, TG], F32)) for i in range(2)]
        ps_o = [EO(nc.psum_tensor(f"a1_ps_o{i}", [128, TG], F32)) for i in range(2)]
        p.dma("gpsimd", lambda e: e.dma_start(
            out=wo_sb[:].rearrange("p (c f) -> p c f", c=8),
            in_=io.hbm["w_out1"].rearrange("(c p) f -> p c f", p=128)),
            reads=[io.hbm["w_out1"]], writes=[wo_sb])
        for mt in range(8):
            for tg in range(8):
                sl = (mt * 8 + tg) % 2
                for ec in range(8):
                    rhs = ctx_all[ec][:].rearrange("p (r j) -> p j r", r=16)[
                        :, tg * 32:(tg + 1) * 32, :]
                    p.op("tensor", lambda e, sl=sl, ec=ec, mt=mt, rhs=rhs: e.matmul(
                        ps_o[sl][:], wo_sb[:, ec * 1024 + mt * 128:ec * 1024 + (mt + 1) * 128],
                        rhs, start=(ec == 0), stop=(ec == 7)),
                        reads=[wo_sb, ctx_all[ec]], writes=[ps_o[sl]])
                p.op("vector", lambda e, sl=sl, mt=mt: e.tensor_scalar(
                    ot[sl][:], ps_o[sl][:], cst.b_out1[:, mt:mt + 1], None, ALU.add),
                    reads=[ps_o[sl], cst.b_out1], writes=[ot[sl]])
                p.dma("sync", lambda e, sl=sl, mt=mt, tg=tg: e.dma_start(
                    out=io.attnout.rearrange("(c p) t -> p c t", p=128)[:, mt, tg * TG:(tg + 1) * TG],
                    in_=ot[sl][:]),
                    reads=[ot[sl]], writes=[io.attnout])
    p.barrier()



_PHASES = {
    "ln0": dict(
        shared=["ones_col", "ones_row", "nw0"],
        per_core=["x"],
        outs={"xacc": ((D, S), np.float32), "h": ((D, S), BF), "xn": ((D, S), BF)},
        consts=["ones_col", "ones_row", "nw0"]),
    "ln1": dict(
        shared=["ones_col", "ones_row"],
        per_core=["x"],
        outs={"xn": ((D, S), BF)},
        consts=["ones_col", "ones_row"]),
    "attn0": dict(
        shared=["w_qkv0", "b_qkv0", "bias_tbl0", "iota_loc", "w_out0", "b_out0",
                "ones_bf", "ident_bf", "cos_tbl", "sin_tbl", "masksneg"],
        per_core=["xn"],
        outs={"attnout": ((D, S), np.float32)},
        consts=["ones_bf", "ident_bf", "cos_tbl", "sin_tbl", "masksneg", "b_out0"]),
    "attn1": dict(
        shared=["wqk1", "wv1", "bqk1", "bv1", "w_out1", "b_out1", "ones_bf", "ident_bf"],
        per_core=["xn"],
        outs={"attnout": ((D, S), np.float32)},
        consts=["ones_bf", "ident_bf", "bqk1", "bv1", "b_out1"]),
    "ffn0": dict(
        shared=["gate_w0", "gate_b0", "w1t0", "b1t0", "w2t0", "b2_0", "sel8",
                "ones_bf", "ident_bf"],
        per_core=["h", "xacc", "attnout"],
        outs={"x_out": ((D, S), np.float32), "dbg_ct": ((8, S), BF),
              "dbg_lg": ((128, 128), np.float32)},
        consts=["sel8", "ones_bf", "ident_bf"]),
    "ffn1": dict(
        shared=["wg1", "w1_1", "w2_1", "sel8", "ones_bf", "ident_bf"],
        per_core=["h", "xacc", "attnout"],
        outs={"x_out": ((D, S), np.float32)},
        consts=["sel8", "ones_bf", "ident_bf"]),
}


def build_phase(phase, hostd, per_core_shapes):
    cfg = _PHASES[phase]
    nc = bass.Bass("TRN2", target_bir_lowering=False)
    aps = {}
    for name in cfg["shared"]:
        arr = hostd[name]
        dt = BF16 if arr.dtype == BF else F32
        aps[name] = nc.dram_tensor(name, list(arr.shape), dt, kind="ExternalInput").ap()
    for name in cfg["per_core"]:
        shape, dtype = per_core_shapes[name]
        dt = BF16 if dtype == BF else F32
        aps[name] = nc.dram_tensor(name, list(shape), dt, kind="ExternalInput").ap()
    for name, (shape, dtype) in cfg["outs"].items():
        dt = BF16 if dtype == BF else F32
        aps[name] = nc.dram_tensor(name, list(shape), dt, kind="ExternalOutput").ap()
    p = Prog(nc)
    ctx = ExitStack()
    with ctx:
        cst = emit_consts(p, ctx, aps, cfg["consts"])
        io = NS(hbm=aps,
                x_in=aps.get("x"), xn=aps.get("xn"), xacc=aps.get("xacc"),
                h=aps.get("h"), attnout=aps.get("attnout"), x_out=aps.get("x_out"),
                dbg_ct=aps.get("dbg_ct"), dbg_lg=aps.get("dbg_lg"))
        if phase == "ln0":
            emit_ln(p, ctx, io, cst, block0=True)
        elif phase == "ln1":
            emit_ln(p, ctx, io, cst, block0=False)
        elif phase == "attn0":
            emit_attn0(p, ctx, io, cst)
        elif phase == "attn1":
            emit_attn1(p, ctx, io, cst)
        elif phase == "ffn0":
            emit_ffn(p, ctx, io, cst, block0=True)
        elif phase == "ffn1":
            emit_ffn(p, ctx, io, cst, block0=False)
        p.barrier()
        p.finalize(ctx)
    return nc


def run_phase(phase, hostd, per_core, n_cores=8, **kw):
    cfg = _PHASES[phase]
    shapes = {k: (v[0].shape, v[0].dtype.type if v[0].dtype != np.dtype(BF) else BF)
              for k, v in per_core.items()}
    shapes = {k: (v[0].shape, BF if v[0].dtype == np.dtype(BF) else np.float32)
              for k, v in per_core.items()}
    nc = build_phase(phase, hostd, shapes)
    shared = {k: hostd[k] for k in cfg["shared"]}
    in_maps = []
    for b in range(n_cores):
        m = dict(shared)
        for k, v in per_core.items():
            m[k] = np.ascontiguousarray(v[b])
        in_maps.append(m)
    res = run_bass_kernel_spmd(nc, in_maps, core_ids=list(range(n_cores)), **kw)
    return res



_ALL_SHARED = ["w_qkv0", "b_qkv0", "bias_tbl0", "iota_loc", "w_out0", "b_out0",
               "ones_col", "ones_row", "nw0", "ones_bf", "ident_bf", "cos_tbl",
               "sin_tbl", "masksneg", "gate_w0", "gate_b0", "w1t0", "b1t0",
               "w2t0", "b2_0", "sel8", "wqk1", "wv1", "bqk1", "bv1", "w_out1",
               "b_out1", "wg1", "w1_1", "w2_1"]
_ALL_CONSTS = ["ones_col", "ones_row", "nw0", "ones_bf", "ident_bf", "cos_tbl",
               "sin_tbl", "masksneg", "b_out0", "sel8", "bqk1", "bv1", "b_out1"]


def _uniquify_names(nc):
    ctr = [0]
    orig_s, orig_p = nc.sbuf_tensor, nc.psum_tensor

    def s(name, *a, **k):
        ctr[0] += 1
        return orig_s(f"{name}_u{ctr[0]}", *a, **k)

    def q(name, *a, **k):
        ctr[0] += 1
        return orig_p(f"{name}_u{ctr[0]}", *a, **k)

    nc.sbuf_tensor = s
    nc.psum_tensor = q


def build_full(hostd):
    nc = bass.Bass("TRN2", target_bir_lowering=False)
    _uniquify_names(nc)
    aps = {}
    for name in _ALL_SHARED:
        arr = hostd[name]
        dt = BF16 if arr.dtype == BF else F32
        aps[name] = nc.dram_tensor(name, list(arr.shape), dt, kind="ExternalInput").ap()
    aps["x"] = nc.dram_tensor("x", [D, S], F32, kind="ExternalInput").ap()
    aps["y"] = nc.dram_tensor("y", [D, S], F32, kind="ExternalOutput").ap()
    scr = {}
    scr["xacc"] = nc.dram_tensor("s_xacc", [D, S], F32, kind="Internal").ap()
    scr["h0"] = nc.dram_tensor("s_h0", [D, S], BF16, kind="Internal").ap()
    scr["xn0"] = nc.dram_tensor("s_xn0", [D, S], BF16, kind="Internal").ap()
    scr["att0"] = nc.dram_tensor("s_att0", [D, S], F32, kind="Internal").ap()
    scr["x1"] = nc.dram_tensor("s_x1", [D, S], F32, kind="Internal").ap()
    scr["xn1"] = nc.dram_tensor("s_xn1", [D, S], BF16, kind="Internal").ap()
    scr["att1"] = nc.dram_tensor("s_att1", [D, S], F32, kind="Internal").ap()

    p = Prog(nc)
    with ExitStack() as ctx:
        base = dict(hbm=aps, dbg_ct=None, dbg_lg=None)
        with ExitStack() as c0:
            cst = emit_consts(p, c0, aps, _PHASES["ln0"]["consts"])
            emit_ln(p, c0, NS(**base, x_in=aps["x"], xn=scr["xn0"], xacc=scr["xacc"],
                              h=scr["h0"]), cst, block0=True)
            p.barrier()
        with ExitStack() as c1:
            cst = emit_consts(p, c1, aps, _PHASES["attn0"]["consts"])
            emit_attn0(p, c1, NS(**base, xn=scr["xn0"], attnout=scr["att0"]), cst)
            p.barrier()
        with ExitStack() as c2:
            cst = emit_consts(p, c2, aps, _PHASES["ffn0"]["consts"])
            emit_ffn(p, c2, NS(**base, h=scr["h0"], xacc=scr["xacc"],
                               attnout=scr["att0"], x_out=scr["x1"]), cst, block0=True)
            p.barrier()
        with ExitStack() as c3:
            cst = emit_consts(p, c3, aps, _PHASES["ln1"]["consts"])
            emit_ln(p, c3, NS(**base, x_in=scr["x1"], xn=scr["xn1"]), cst, block0=False)
            p.barrier()
        with ExitStack() as c4:
            cst = emit_consts(p, c4, aps, _PHASES["attn1"]["consts"])
            emit_attn1(p, c4, NS(**base, xn=scr["xn1"], attnout=scr["att1"]), cst)
            p.barrier()
        with ExitStack() as c5:
            cst = emit_consts(p, c5, aps, _PHASES["ffn1"]["consts"])
            emit_ffn(p, c5, NS(**base, h=scr["xn1"], xacc=scr["x1"],
                               attnout=scr["att1"], x_out=aps["y"]), cst, block0=False)
            p.barrier()
        p.finalize(ctx)
    return nc


def kernel_fused(inputs):
    hostd = host_prep(inputs)
    x = inputs["L0_attn_x"].astype(np.float32, copy=False)
    n = 8
    nc = build_full(hostd)
    shared = {k: hostd[k] for k in _ALL_SHARED}
    in_maps = []
    for b in range(n):
        m = dict(shared)
        m["x"] = np.ascontiguousarray(x[b].T)
        in_maps.append(m)
    res = run_bass_kernel_spmd(nc, in_maps, core_ids=list(range(n)))
    return np.stack([np.ascontiguousarray(r["y"].T) for r in res.results])


def kernel(**inputs):
    inputs = {k: np.asarray(v) for k, v in inputs.items()}
    return kernel_fused(inputs)


def kernel_multilaunch(**inputs):
    inputs = {k: np.asarray(v) for k, v in inputs.items()}
    hostd = host_prep(inputs)
    x = inputs["L0_attn_x"].astype(np.float32, copy=False)
    n = 8
    xT = [np.ascontiguousarray(x[b].T) for b in range(n)]
    r0 = run_phase("ln0", hostd, {"x": xT}).results
    r1 = run_phase("attn0", hostd, {"xn": [r["xn"] for r in r0]}).results
    r2 = run_phase("ffn0", hostd, {"h": [r["h"] for r in r0],
                                   "xacc": [r["xacc"] for r in r0],
                                   "attnout": [r["attnout"] for r in r1]}).results
    x1 = [r["x_out"] for r in r2]
    r3 = run_phase("ln1", hostd, {"x": x1}).results
    xn1 = [r["xn"] for r in r3]
    r4 = run_phase("attn1", hostd, {"xn": xn1}).results
    r5 = run_phase("ffn1", hostd, {"h": xn1, "xacc": x1,
                                   "attnout": [r["attnout"] for r in r4]}).results
    return np.stack([np.ascontiguousarray(r["x_out"].T) for r in r5])
